# version 06
# baseline (speedup 1.0000x reference)
import os
import numpy as np
import ml_dtypes
import concourse.bass as bass
import concourse.mybir as mybir
from concourse.bass_utils import run_bass_kernel_spmd

F32 = mybir.dt.float32
BF16 = mybir.dt.bfloat16
I32 = mybir.dt.int32
AF = mybir.ActivationFunctionType
AX = mybir.AxisListType
OP = mybir.AluOpType
BF = ml_dtypes.bfloat16

S = 4096
D = 4096
HC = 32
NTB = 8
KOC = 8
KVH = 8
NCH = 32
SCALE = 1.0 / float(np.sqrt(128.0))
CSHIFT = -8.0
EPS_LN = 1e-5
EPS_MOE = 1e-12
LORA_SCALING = 16.0 / 8.0

DEBUG_TAPS = os.environ.get("MK_DEBUG_TAPS", "0") == "1"



def _tile_w(w):
    O, H = w.shape
    return np.ascontiguousarray(
        w.reshape(O // 128, 128, H // 128, 128).transpose(0, 3, 2, 1))


def _consts():
    pos = np.arange(S, dtype=np.float32)[:, None]
    freqs = np.exp(-np.arange(0, 128, 2, dtype=np.float32)
                   * (np.log(10000.0) / 128.0))[None, :]
    ang = pos * freqs
    cos = np.concatenate([np.cos(ang), np.cos(ang)], axis=1).T
    sin = np.concatenate([np.sin(ang), np.sin(ang)], axis=1).T
    rot = np.zeros((128, 128), np.float32)
    for dp in range(64):
        rot[dp + 64, dp] = -1.0
        rot[dp, dp + 64] = 1.0
    ones4v = np.zeros((128, 16), np.float32)
    for qi in range(4):
        ones4v[:, qi * 4 + qi] = 1.0
    ebc = np.zeros((4, 512), np.float32)
    for qi in range(4):
        ebc[qi, qi * 128:(qi + 1) * 128] = 1.0
    indm = np.zeros((16, 128), np.float32)
    for e in range(16):
        indm[e, e * 8:(e + 1) * 8] = 1.0
    return {
        "cosK": np.ascontiguousarray(cos).astype(BF),
        "sinK": np.ascontiguousarray(sin).astype(BF),
        "rotR": rot.astype(BF), "ones4v": ones4v.astype(BF),
        "onescol_f": np.ones((128, 1), np.float32),
        "ones1_f": np.ones((1, 128), np.float32),
        "ones16": np.ones((16, 1), np.float32),
        "ebc": ebc, "indm": indm,
        "iota16": np.arange(16, dtype=np.float32)[:, None],
    }


def _core_blocks(c):
    j = c % 4
    return c // 4, 512 * j, 512 * (7 - j)


def _masks_for(q0s):
    qt = np.arange(512)[None, :]
    out = np.empty((64, 128, 512), np.float32)
    for bl, q0 in enumerate(q0s):
        for ch in range(NCH):
            kt = np.arange(128)[:, None] + ch * 128
            out[bl * 32 + ch] = (q0 + qt) >= kt
    return out.reshape(64 * 128, 512).astype(BF)


def _prep_in_maps(inp):
    x = np.asarray(inp["L0_attn_hidden_states"], np.float32)
    con = _consts()
    xT = [np.ascontiguousarray(x[b].T) for b in range(2)]
    shared = {
        "wq_h": _tile_w(np.asarray(inp["L0_attn_Wq"], np.float32)),
        "wb_h": _tile_w(np.asarray(inp["L0_ffn_W_base"], np.float32)),
        "wo_h": np.ascontiguousarray(
            np.asarray(inp["L0_attn_Wo"], np.float32)
            .reshape(32, 128, 32, 128).transpose(0, 3, 2, 1)),
        "wk_h": np.ascontiguousarray(
            np.asarray(inp["L0_attn_Wk"], np.float32).reshape(8, 128, 32, 128)
            .transpose(3, 2, 0, 1).reshape(128, 32, 1024)),
        "wv_h": np.ascontiguousarray(
            np.asarray(inp["L0_attn_Wv"], np.float32).reshape(8, 128, 32, 128)
            .transpose(3, 2, 0, 1).reshape(128, 32, 1024)),
        "rw_h": np.ascontiguousarray(
            np.asarray(inp["L0_ffn_router_W"], np.float32).reshape(16, 32, 128)
            .transpose(2, 1, 0)),
        "acat_h": np.ascontiguousarray(
            np.asarray(inp["L0_ffn_lora_A"], np.float32).reshape(128, 32, 128)
            .transpose(2, 1, 0)),
        "bcat_h": np.ascontiguousarray(
            np.asarray(inp["L0_ffn_lora_B"], np.float32).transpose(0, 2, 1)
            .reshape(128, 4096)),
        "bb_h": np.ascontiguousarray(
            np.asarray(inp["L0_ffn_b_base"], np.float32).reshape(32, 128).T),
        "rb_h": np.asarray(inp["L0_ffn_router_b"], np.float32),
        "prior_h": np.asarray(inp["L0_ffn_domain_prior"], np.float32),
        "act_h": np.asarray(inp["L0_ffn_active"], np.int32),
    }
    shared.update(con)
    maps = []
    for c in range(8):
        b, q0a, q0b = _core_blocks(c)
        m = dict(shared)
        m["xT"] = xT[b]
        m["xownT"] = np.ascontiguousarray(np.concatenate(
            [xT[b][:, q0a:q0a + 512], xT[b][:, q0b:q0b + 512]], axis=1))
        m["cosQ"] = np.ascontiguousarray(np.concatenate(
            [con["cosK"][:, q0a:q0a + 512], con["cosK"][:, q0b:q0b + 512]], axis=1))
        m["sinQ"] = np.ascontiguousarray(np.concatenate(
            [con["sinK"][:, q0a:q0a + 512], con["sinK"][:, q0b:q0b + 512]], axis=1))
        m["masks"] = _masks_for((q0a, q0b))
        maps.append(m)
    return maps



class _Waits:
    def __init__(self):
        self.seen = {}

    def w(self, eng, ename, sem, val):
        if val <= 0:
            return
        key = (ename, sem.name)
        if self.seen.get(key, 0) >= val:
            return
        eng.wait_ge(sem, val)
        self.seen[key] = val


def _sl(tensor, off, n):
    return tensor[:, off:off + n]


def build_program():
    nc = bass.Bass("TRN2", target_bir_lowering=False)

    dt = {}

    def din(name, shape, dty=F32):
        dt[name] = nc.dram_tensor(name, list(shape), dty, kind="ExternalInput")

    din("xT", (D, S)); din("xownT", (D, 1024))
    din("wq_h", (32, 128, 32, 128)); din("wb_h", (32, 128, 32, 128))
    din("wo_h", (32, 128, 32, 128))
    din("wk_h", (128, 32, 1024)); din("wv_h", (128, 32, 1024))
    din("rw_h", (128, 32, 16)); din("acat_h", (128, 32, 128))
    din("bcat_h", (128, 4096)); din("bb_h", (128, 32))
    din("rb_h", (16,)); din("prior_h", (16,)); din("act_h", (4,), I32)
    din("cosK", (128, S), BF16); din("sinK", (128, S), BF16)
    din("cosQ", (128, 1024), BF16); din("sinQ", (128, 1024), BF16)
    din("rotR", (128, 128), BF16); din("ones4v", (128, 16), BF16)
    din("onescol_f", (128, 1)); din("ones1_f", (1, 128)); din("ones16", (16, 1))
    din("ebc", (4, 512)); din("indm", (16, 128)); din("iota16", (16, 1))
    din("masks", (64 * 128, 512), BF16)

    out_d = nc.dram_tensor("out", [D, 1024], F32, kind="ExternalOutput")
    kdram = nc.dram_tensor("kdram", [1024, S], BF16, kind="Internal")
    vdram = nc.dram_tensor("vdram", [KVH, 128, 32, 128], BF16, kind="Internal")
    x1dram = nc.dram_tensor("x1dram", [D, 1024], F32, kind="Internal")
    taps = {}
    if DEBUG_TAPS:
        taps["tapK"] = nc.dram_tensor("tapK", [1024, S], BF16, kind="ExternalOutput")
        taps["tapV"] = nc.dram_tensor("tapV", [KVH, 128, 32, 128], BF16,
                                      kind="ExternalOutput")
        taps["tapQ"] = nc.dram_tensor("tapQ", [128, 32768], BF16,
                                      kind="ExternalOutput")
        taps["tapOT"] = nc.dram_tensor("tapOT", [128, 32768], BF16,
                                       kind="ExternalOutput")
        taps["tapX1"] = nc.dram_tensor("tapX1", [D, 1024], F32,
                                       kind="ExternalOutput")
        taps["tapA"] = nc.dram_tensor("tapA", [16, 528], F32,
                                      kind="ExternalOutput")

    sems = {}
    sem_names = [
        "s0", "sp_v", "sp_s", "sp_t",
        "a1w", "a1x", "a1mm", "a1kraw", "a1rot", "a1tt1", "a1cmb", "a1kst",
        "a2w", "a2x", "a2mm", "a2ev", "a2vst",
        "b1x", "b1w", "b1mm", "b1qraw", "b1rot", "b1tt1", "b1cmb",
        "b2k0", "b2k1", "b2v0", "b2v1", "b2m0", "b2m1", "b2sc", "b2ex", "b2pl", "b2pv", "b2sm",
        "b2rc", "b2bc", "b2bb", "b2stt",
        "b3w", "b3x", "b3mm", "b3rs", "b3sq", "b3s1", "b3s2",
        "b3me", "b3e2", "b3var", "b3rstd", "b3bc", "b3mbe", "b3x1", "b3xs",
        "b4wb", "b4x", "b4mm", "b4ye", "b4r2", "b4sq", "b4s1", "b4s2",
        "b4r", "b4z", "b4se", "b4sm", "b4iv", "b4b16", "b4al", "b4aem",
        "b4ze", "b4w", "b4me", "b4e2", "b4var", "b4rstd", "b4bc", "b4mbe",
        "b4o", "sout", "stap", "vch",
    ]
    for name in sem_names:
        cm = nc.semaphore(name)
        sems[name] = cm.__enter__()

    nc.alloc_sbuf_tensor("mk_reserved", [128, 211872], mybir.dt.dt.uint8
                         if hasattr(mybir.dt, "dt") else mybir.dt.uint8)
    cursor = [16512]

    def at(name, shape, dty, off=None):
        if off is None:
            off = cursor[0]
        nbytes = int(np.prod(shape[1:])) * {F32: 4, BF16: 2, I32: 4}[dty]
        h = nc.alloc_sbuf_tensor_at(name, list(shape), dty, offset=off)
        cursor[0] = off + ((nbytes + 31) // 32) * 32
        return h

    rotR = at("rotR_sb", [128, 128], BF16)
    ones4v = at("ones4v_sb", [128, 16], BF16)
    onescol = at("onescol_sb", [128, 1], F32)
    ones1f = at("ones1f_sb", [1, 128], F32)
    ones16 = at("ones16_sb", [16, 1], F32)
    ebc = at("ebc_sb", [4, 512], F32)
    indm = at("indm_sb", [16, 128], F32)
    iota16 = at("iota16_sb", [16, 1], F32)
    negC = at("negC_sb", [128, 1], F32)
    eps16 = at("eps16_sb", [16, 1], F32)
    eps1 = at("eps1_sb", [1, 1], F32)
    rb16 = at("rb16_sb", [16, 1], F32)
    prior16 = at("prior16_sb", [16, 1], F32)
    act_i = at("act_i_sb", [1, 4], I32)
    act_f = at("act_f_sb", [1, 4], F32)
    eq16 = at("eq16_sb", [16, 4], F32)
    mask16 = at("mask16_sb", [16, 1], F32)
    lnp16 = at("lnp16_sb", [16, 1], F32)
    t16a = at("t16a_sb", [16, 1], F32)
    t16b = at("t16b_sb", [16, 1], F32)
    biasv = at("biasv_sb", [16, 1], F32)
    mS = at("m_sb", [1, 512], F32)
    rstdS = at("rstd_sb", [1, 512], F32)
    tA5 = at("tA5_sb", [1, 512], F32)
    tB5 = at("tB5_sb", [1, 512], F32)
    arena2 = at("arena2", [128, 32768], BF16)
    PHASE0 = cursor[0]

    wkv = at("wkv_sb", [128, HC * 1024], BF16, PHASE0)
    kvout = at("kvout_sb", [128, 2 * 4096], BF16)
    krawA = at("krawA", [128, 2 * 512], BF16)
    c1A = at("c1A", [128, 2 * 512], F32)
    c2A = at("c2A", [128, 2 * 512], F32)
    cosK = at("cosK_sb", [128, S], BF16)
    sinK = at("sinK_sb", [128, S], BF16)

    qT = at("qT_sb", [128, 32 * 1024], BF16, PHASE0)
    PH_B1 = PHASE0 + 65536
    cosQ = at("cosQ_sb", [128, 1024], BF16, PH_B1)
    sinQ = at("sinQ_sb", [128, 1024], BF16)
    wq_ch = at("wq_ch", [128, 2 * 4096], BF16)
    krawB = at("krawB", [128, 2 * 512], BF16)
    c1B = at("c1B", [128, 2 * 512], F32)
    c2B = at("c2B", [128, 2 * 512], F32)

    Kb = at("Kb_sb", [128, 2 * 4096], BF16, PH_B1)
    Vb = at("Vb_sb", [128, 2 * 4096], BF16)
    Pt = at("Pt_sb", [128, 8 * 512], BF16)
    Pm = at("Pm_sb", [128, 4 * 512], BF16)
    mkb = at("mkb_sb", [128, 2 * 512], BF16)
    inv4 = at("inv4_sb", [4, 512], F32)
    invb = at("invb_sb", [128, 512], F32)

    resid = at("resid_sb", [128, 32 * 512], F32, PHASE0)
    wo_ch = at("wo_ch", [128, 2 * 4096], BF16, PH_B1)
    xf = at("xf_sb", [128, 2 * 512], F32)
    t1f = at("t1f_sb", [128, 2 * 512], F32)
    x1f = at("x1f_sb", [128, 2 * 512], F32)
    sqv = at("sq_sb", [128, 2 * 512], F32)
    mb2 = at("mb2_sb", [128, 512], F32)
    rb2 = at("rb2_sb", [128, 512], F32)

    r2 = at("r2_sb", [128, 32 * 512], F32, PHASE0)
    wb_ch = at("wb_ch", [128, 2 * 4096], BF16, PH_B1)
    bcat = at("bcat_sb", [128, 4096], BF16)
    acat = at("acat_sb", [128, 4096], BF16)
    rw = at("rw_sb", [128, 512], BF16)
    bbt = at("bbt_sb", [128, 32], F32)
    xf1 = at("xf1_sb", [128, 2 * 512], F32)
    fY = at("fY_sb", [128, 2 * 512], F32)
    sq2 = at("sq2_sb", [128, 2 * 512], F32)
    outf = at("outf_sb", [128, 2 * 512], F32)
    t1g = at("t1g_sb", [128, 2 * 512], F32)
    wbf = at("wbf_sb", [128, 2 * 512], BF16)
    zsb = at("z_sb", [128, 512], F32)
    se = at("se_sb", [16, 512], F32)
    alpha = at("alpha_sb", [16, 512], F32)
    inv1 = at("inv1_sb", [1, 512], F32)
    mb4 = at("mb4_sb", [128, 512], F32)
    rb4 = at("rb4_sb", [128, 512], F32)
    assert cursor[0] <= 229344, f"SBUF over budget: {cursor[0]}"
    print("MK sbuf map end:", cursor[0])

    ps = []
    for i in range(8):
        cm = nc.psum_tensor(f"ps{i}", [128, 512], F32)
        ps.append(cm.__enter__())

    W = _Waits()
    NCONST = 13
    S0F = 16 * NCONST
    S0B1 = S0F + 32
    S0B4 = S0B1 + 64

    def xa_sl(tb, hc, lo=0, n=512):
        base = (tb % 2) * 16384 + hc * 512 + lo
        return arena2[:, base:base + n]

    def ph0_sync(y):
        loads = [
            (cosK, dt["cosK"][:]), (sinK, dt["sinK"][:]),
            (rotR, dt["rotR"][:]), (ones4v, dt["ones4v"][:]),
            (onescol, dt["onescol_f"][:]), (ones1f, dt["ones1_f"][:]),
            (ones16, dt["ones16"][:]), (ebc, dt["ebc"][:]),
            (indm, dt["indm"][:]), (iota16, dt["iota16"][:]),
            (rb16, dt["rb_h"][:, None]), (prior16, dt["prior_h"][:, None]),
            (act_i, dt["act_h"][None, :]),
        ]
        assert len(loads) == NCONST
        for sb, ap in loads:
            y.dma_start(out=sb[:], in_=ap).then_inc(sems["s0"], 16)

    vcnt = [0]

    def vsync(v, ins):
        vcnt[0] += 1
        ins.then_inc(sems["vch"], 1)
        v.wait_ge(sems["vch"], vcnt[0])

    def phP_v(v):
        v.memset(negC[:], CSHIFT).then_inc(sems["sp_v"], 1)
        v.memset(eps16[:], EPS_MOE).then_inc(sems["sp_v"], 1)
        v.memset(eps1[:], EPS_LN).then_inc(sems["sp_v"], 1)
        W.w(v, "v", sems["s0"], S0F)
        v.tensor_copy(act_f[:], act_i[:]).then_inc(sems["sp_v"], 1)
        W.w(v, "v", sems["sp_t"], 1)
        vsync(v, v.tensor_single_scalar(eq16[:], ps[0][0:16, 0:4], iota16[:],
                                        OP.is_equal))
        v.sem_inc(sems["sp_v"], 1)
        vsync(v, v.tensor_add(t16a[:], eq16[:, 0:1], eq16[:, 1:2]))
        v.sem_inc(sems["sp_v"], 1)
        vsync(v, v.tensor_add(t16b[:], eq16[:, 2:3], eq16[:, 3:4]))
        v.sem_inc(sems["sp_v"], 1)
        vsync(v, v.tensor_add(mask16[:], t16a[:], t16b[:]))
        v.sem_inc(sems["sp_v"], 1)
        vsync(v, v.tensor_scalar(t16a[:], mask16[:], 1.0, 1e9, OP.subtract,
                                 OP.mult))
        v.sem_inc(sems["sp_v"], 1)
        W.w(v, "v", sems["sp_s"], 1)
        vsync(v, v.tensor_add(t16b[:], lnp16[:], rb16[:]))
        v.sem_inc(sems["sp_v"], 1)
        vsync(v, v.tensor_add(biasv[:], t16a[:], t16b[:]))
        v.sem_inc(sems["sp_v"], 1)

    def phP_s(s):
        W.w(s, "s", sems["s0"], S0F)
        W.w(s, "s", sems["sp_v"], 2)
        s.activation(lnp16[:], prior16[:], AF.Ln,
                     bias=eps16[:]).then_inc(sems["sp_s"], 1)

    def phP_t(t):
        W.w(t, "t", sems["s0"], S0F)
        W.w(t, "t", sems["sp_v"], 4)
        t.matmul(ps[0][0:16, 0:4], ones1f[:, 0:16], act_f[:],
                 start=True, stop=True).then_inc(sems["sp_t"], 1)

    def phA1_g(g):
        g.dma_start(out=wkv[:], in_=dt["wk_h"][:]).then_inc(sems["a1w"], 16)
        for tb in range(NTB):
            W.w(g, "g", sems["a1mm"], 8 * (tb - 1))
            g.dma_start(
                out=_sl(arena2, (tb % 2) * 16384, 16384),
                in_=dt["xT"][:, tb * 512:(tb + 1) * 512]
                .rearrange("(c p) t -> p c t", p=128),
            ).then_inc(sems["a1x"], 16)

    def phA1_t(t):
        W.w(t, "t", sems["sp_v"], 5)
        W.w(t, "t", sems["a1w"], 16)

        def rot(gk):
            W.w(t, "t", sems["a1kraw"], gk + 1)
            W.w(t, "t", sems["a1cmb"], gk - 1)
            t.matmul(ps[2 + gk % 2][:], rotR[:],
                     _sl(krawA, (gk % 2) * 512, 512),
                     start=True, stop=True).then_inc(sems["a1rot"], 1)

        for tb in range(NTB):
            W.w(t, "t", sems["a1x"], 16 * (tb + 1))
            for koc in range(KOC):
                gk = tb * 8 + koc
                W.w(t, "t", sems["a1kraw"], gk - 1)
                for hc in range(HC):
                    ins = t.matmul(
                        ps[gk % 2][:],
                        _sl(wkv, hc * 1024 + koc * 128, 128),
                        xa_sl(tb, hc),
                        start=(hc == 0), stop=(hc == HC - 1))
                ins.then_inc(sems["a1mm"], 1)
                if gk >= 1:
                    rot(gk - 1)
        rot(NTB * KOC - 1)

    def phA1_s(s):
        for gk in range(NTB * KOC):
            W.w(s, "s", sems["a1mm"], gk + 1)
            W.w(s, "s", sems["a1rot"], gk - 1)
            W.w(s, "s", sems["a1tt1"], gk - 1)
            s.activation(_sl(krawA, (gk % 2) * 512, 512),
                         ps[gk % 2][:], AF.Identity).then_inc(sems["a1kraw"], 1)

    def phA1_v(v):
        for gk in range(NTB * KOC):
            tb, koc = gk // 8, gk % 8
            b = gk % 2
            W.w(v, "v", sems["a1kraw"], gk + 1)
            v.tensor_mul(_sl(c1A, b * 512, 512), _sl(krawA, b * 512, 512),
                         cosK[:, tb * 512:tb * 512 + 512]
                         ).then_inc(sems["a1tt1"], 1)
            W.w(v, "v", sems["a1rot"], gk + 1)
            v.scalar_tensor_tensor(_sl(c2A, b * 512, 512), ps[2 + b][:], 1.0,
                                   sinK[:, tb * 512:tb * 512 + 512],
                                   OP.mult, OP.mult)
            if koc == 0:
                W.w(v, "v", sems["a1kst"], 16 * (tb - 1))
            v.tensor_add(_sl(kvout, (tb % 2) * 4096 + koc * 512, 512),
                         _sl(c1A, b * 512, 512),
                         _sl(c2A, b * 512, 512)).then_inc(sems["a1cmb"], 1)

    def phA1_y(y):
        for tb in range(NTB):
            W.w(y, "y", sems["a1cmb"], 8 * (tb + 1))
            y.dma_start(
                out=kdram[:, tb * 512:(tb + 1) * 512]
                .rearrange("(c p) t -> p c t", p=128),
                in_=_sl(kvout, (tb % 2) * 4096, 4096),
            ).then_inc(sems["a1kst"], 16)

    def phA2_g(g):
        W.w(g, "g", sems["a1mm"], 64)
        g.dma_start(out=wkv[:], in_=dt["wv_h"][:]).then_inc(sems["a2w"], 16)
        for tb in range(NTB):
            W.w(g, "g", sems["a2mm"], 8 * (tb - 1))
            g.dma_start(
                out=_sl(arena2, (tb % 2) * 16384, 16384),
                in_=dt["xT"][:, tb * 512:(tb + 1) * 512]
                .rearrange("(c p) t -> p c t", p=128),
            ).then_inc(sems["a2x"], 16)

    def phA2_t(t):
        W.w(t, "t", sems["a1kraw"], 64)
        W.w(t, "t", sems["a2w"], 16)
        for tb in range(NTB):
            W.w(t, "t", sems["a2x"], 16 * (tb + 1))
            for u in range(8):
                tc, dvb = u // 2, u % 2
                gk = tb * 8 + u
                W.w(t, "t", sems["a2ev"], gk - 1)
                for hc in range(HC):
                    ins = t.matmul(
                        ps[gk % 2][:],
                        xa_sl(tb, hc, tc * 128, 128),
                        _sl(wkv, hc * 1024 + dvb * 512, 512),
                        start=(hc == 0), stop=(hc == HC - 1))
                ins.then_inc(sems["a2mm"], 1)

    def phA2_s(s):
        vview = kvout[:].rearrange("p (w k tc d) -> p w k tc d", w=2, k=8, tc=4)
        for gk in range(NTB * 8):
            tb, u = gk // 8, gk % 8
            tc, dvb = u // 2, u % 2
            W.w(s, "s", sems["a2mm"], gk + 1)
            if u == 0:
                W.w(s, "s", sems["a1kst"], 128)
                W.w(s, "s", sems["a2vst"], 16 * 8 * (tb - 1))
            s.activation(
                vview[:, tb % 2:tb % 2 + 1, dvb * 4:dvb * 4 + 4, tc:tc + 1, :],
                ps[gk % 2][:], AF.Identity).then_inc(sems["a2ev"], 1)

    def phA2_y(y):
        vview = kvout[:].rearrange("p (w k tc d) -> p w k tc d", w=2, k=8, tc=4)
        for tb in range(NTB):
            W.w(y, "y", sems["a2ev"], 8 * (tb + 1))
            for kvh in range(KVH):
                y.dma_start(
                    out=vdram[kvh:kvh + 1, :, tb * 4:(tb + 1) * 4, :],
                    in_=vview[:, tb % 2:tb % 2 + 1, kvh:kvh + 1, :, :],
                ).then_inc(sems["a2vst"], 16)

    A_DONE = [("a2mm", 64), ("a1cmb", 64), ("a1kst", 128), ("a2vst", 1024),
              ("a2ev", 64)]

    def phB1_y(y):
        for sname, val in A_DONE:
            W.w(y, "y", sems[sname], val)
        y.dma_start(out=cosQ[:], in_=dt["cosQ"][:]).then_inc(sems["s0"], 16)
        y.dma_start(out=sinQ[:], in_=dt["sinQ"][:]).then_inc(sems["s0"], 16)

    def phB1_g(g):
        for sname, val in A_DONE:
            W.w(g, "g", sems[sname], val)
        for i in range(4):
            g.dma_start(
                out=_sl(arena2, i * 8192, 8192),
                in_=dt["xownT"][i * 1024:(i + 1) * 1024, :]
                .rearrange("(c p) t -> p c t", p=128),
            ).then_inc(sems["b1x"], 16)
        for oc in range(32):
            W.w(g, "g", sems["b1mm"], 2 * (oc - 1))
            g.dma_start(out=_sl(wq_ch, (oc % 2) * 4096, 4096),
                        in_=dt["wq_h"][oc]).then_inc(sems["b1w"], 16)

    def phB1_t(t):
        W.w(t, "t", sems["a2ev"], 64)
        W.w(t, "t", sems["a1cmb"], 64)
        W.w(t, "t", sems["b1x"], 64)

        def rot(u):
            W.w(t, "t", sems["b1qraw"], u + 1)
            W.w(t, "t", sems["b1cmb"], u - 1)
            t.matmul(ps[2 + u % 2][:], rotR[:], _sl(krawB, (u % 2) * 512, 512),
                     start=True, stop=True).then_inc(sems["b1rot"], 1)

        for oc in range(32):
            W.w(t, "t", sems["b1w"], 16 * (oc + 1))
            for tt in range(2):
                u = oc * 2 + tt
                W.w(t, "t", sems["b1qraw"], u - 1)
                for hc in range(HC):
                    ins = t.matmul(
                        ps[u % 2][:],
                        _sl(wq_ch, (oc % 2) * 4096 + hc * 128, 128),
                        _sl(arena2, hc * 1024 + tt * 512, 512),
                        start=(hc == 0), stop=(hc == HC - 1))
                ins.then_inc(sems["b1mm"], 1)
                if u >= 1:
                    rot(u - 1)
        rot(63)

    def phB1_s(s):
        for u in range(64):
            W.w(s, "s", sems["b1mm"], u + 1)
            W.w(s, "s", sems["b1rot"], u - 1)
            W.w(s, "s", sems["b1tt1"], u - 1)
            s.activation(_sl(krawB, (u % 2) * 512, 512),
                         ps[u % 2][:], AF.Identity).then_inc(sems["b1qraw"], 1)

    def phB1_v(v):
        W.w(v, "v", sems["s0"], S0B1)
        W.w(v, "v", sems["a2vst"], 1024)
        W.w(v, "v", sems["a1kst"], 128)
        for u in range(64):
            oc, tt = u // 2, u % 2
            b = u % 2
            W.w(v, "v", sems["b1qraw"], u + 1)
            v.tensor_mul(_sl(c1B, b * 512, 512), _sl(krawB, b * 512, 512),
                         _sl(cosQ, tt * 512, 512)).then_inc(sems["b1tt1"], 1)
            W.w(v, "v", sems["b1rot"], u + 1)
            v.scalar_tensor_tensor(_sl(c2B, b * 512, 512), ps[2 + b][:], 1.0,
                                   _sl(sinQ, tt * 512, 512), OP.mult, OP.mult)
            v.tensor_add(_sl(qT, oc * 1024 + tt * 512, 512),
                         _sl(c1B, b * 512, 512),
                         _sl(c2B, b * 512, 512)).then_inc(sems["b1cmb"], 1)

    B1_DONE = [("b1mm", 64), ("b1cmb", 64), ("b1qraw", 64)]
    CHN = [16] * 8 + [32] * 8
    CUM = [0]
    for _c in CHN:
        CUM.append(CUM[-1] + _c)
    TCH = CUM[16]

    def phB2_y(y):
        for sname, val in B1_DONE:
            W.w(y, "y", sems[sname], val)
        for kv in range(16):
            bl, kvh = kv // 8, kv % 8
            if kv >= 2:
                W.w(y, "y", sems["b2sc"], 4 * CUM[kv - 1])
            y.dma_start(out=_sl(Kb, (kv % 2) * 4096, 4096),
                        in_=kdram[kvh * 128:(kvh + 1) * 128, :]
                        ).then_inc(sems["b2k%d" % (kv % 2)], 16)
            if kv >= 2:
                W.w(y, "y", sems["b2pv"], 4 * CUM[kv - 1])
            y.dma_start(out=_sl(Vb, (kv % 2) * 4096, 4096),
                        in_=vdram[kvh:kvh + 1, :, :, :]
                        ).then_inc(sems["b2v%d" % (kv % 2)], 16)
            for c in range(CHN[kv]):
                mi = CUM[kv] + c
                W.w(y, "y", sems["b2pl"], 4 * (mi - 1))
                y.dma_start(out=_sl(mkb, (c % 2) * 512, 512),
                            in_=dt["masks"][(bl * 32 + c) * 128:
                                            (bl * 32 + c + 1) * 128, :]
                            ).then_inc(sems["b2m%d" % (mi % 2)], 16)
        if DEBUG_TAPS:
            W.w(y, "y", sems["b1cmb"], 64)
            y.dma_start(out=taps["tapQ"][:], in_=qT[:]).then_inc(sems["stap"], 16)
            W.w(y, "y", sems["b2stt"], 64)
            y.dma_start(out=taps["tapOT"][:], in_=arena2[:]).then_inc(sems["stap"], 16)

    def phB2_t(t):
        W.w(t, "t", sems["b1qraw"], 64)
        W.w(t, "t", sems["b1cmb"], 64)
        for kv in range(16):
            bl, kvh = kv // 8, kv % 8
            W.w(t, "t", sems["b2k%d" % (kv % 2)], 16 * (kv // 2 + 1))
            W.w(t, "t", sems["b2v%d" % (kv % 2)], 16 * (kv // 2 + 1))
            W.w(t, "t", sems["b2rc"], kv)
            nch = CHN[kv]
            W.w(t, "t", sems["b2bb"], 4 * kv)
            for c in range(nch + 1):
                if c < nch:
                    for qi in range(4):
                        qh = kvh * 4 + qi
                        u2 = (CUM[kv] + c) * 4 + qi
                        W.w(t, "t", sems["b2ex"], u2 - 2)
                        t.matmul(ps[(4, 5, 7)[u2 % 3]][:],
                                 _sl(Kb, (kv % 2) * 4096 + c * 128, 128),
                                 _sl(qT, qh * 1024 + bl * 512, 512),
                                 start=True, stop=True).then_inc(sems["b2sc"], 1)
                if c >= 1:
                    cp = c - 1
                    for qi in range(4):
                        if cp == 0:
                            W.w(t, "t", sems["b2stt"], 4 * (kv - 1) + qi + 1)
                        W.w(t, "t", sems["b2pl"], 4 * (CUM[kv] + cp) + qi + 1)
                        t.matmul(ps[qi][:],
                                 _sl(Vb, (kv % 2) * 4096 + cp * 128, 128),
                                 _sl(Pm, qi * 512, 512),
                                 start=(cp == 0), stop=(cp == nch - 1)
                                 ).then_inc(sems["b2pv"], 1)
                    for qi in range(4):
                        t.matmul(ps[6][0:4, :], ones4v[:, qi * 4:qi * 4 + 4],
                                 _sl(Pm, qi * 512, 512),
                                 start=(cp == 0 and qi == 0),
                                 stop=(cp == nch - 1 and qi == 3)
                                 ).then_inc(sems["b2sm"], 1)
            W.w(t, "t", sems["b2ex"], 4 * CUM[kv + 1])
            for qi in range(4):
                W.w(t, "t", sems["b2rc"], kv + 1)
                W.w(t, "t", sems["b2bb"], 4 * kv + qi)
                t.matmul(ps[7][:], ebc[:, qi * 128:qi * 128 + 128], inv4[:],
                         start=True, stop=True).then_inc(sems["b2bc"], 1)

    def phB2_s(s):
        for kv in range(16):
            for c in range(CHN[kv]):
                for qi in range(4):
                    u2 = (CUM[kv] + c) * 4 + qi
                    W.w(s, "s", sems["b2sc"], u2 + 1)
                    W.w(s, "s", sems["b2pl"], 4 * (CUM[kv] + c - 2) + qi + 1)
                    s.activation(_sl(Pt, (qi * 2 + c % 2) * 512, 512),
                                 ps[(4, 5, 7)[u2 % 3]][:],
                                 AF.Exp, bias=negC[:], scale=SCALE
                                 ).then_inc(sems["b2ex"], 1)

    def phB2_g(g):
        pass

    def phB2_v(v):
        for kv in range(16):
            bl, kvh = kv // 8, kv % 8
            for c in range(CHN[kv]):
                for qi in range(4):
                    u2 = (CUM[kv] + c) * 4 + qi
                    W.w(v, "v", sems["b2ex"], u2 + 1)
                    mi_ = CUM[kv] + c
                    W.w(v, "v", sems["b2m%d" % (mi_ % 2)], 16 * (mi_ // 2 + 1))
                    W.w(v, "v", sems["b2pv"], 4 * (CUM[kv] + c - 1) + qi + 1)
                    W.w(v, "v", sems["b2sm"], 4 * (CUM[kv] + c - 1) + qi + 1)
                    v.tensor_mul(_sl(Pm, qi * 512, 512),
                                 _sl(Pt, (qi * 2 + c % 2) * 512, 512),
                                 _sl(mkb, (c % 2) * 512, 512)
                                 ).then_inc(sems["b2pl"], 1)
            W.w(v, "v", sems["b2sm"], 4 * CUM[kv + 1])
            W.w(v, "v", sems["b2bc"], 4 * kv)
            v.reciprocal(inv4[:], ps[6][0:4, :]).then_inc(sems["b2rc"], 1)
            for qi in range(4):
                qh = kvh * 4 + qi
                W.w(v, "v", sems["b2bc"], 4 * kv + qi + 1)
                v.tensor_copy(_sl(invb, 0, 512),
                              ps[7][:]).then_inc(sems["b2bb"], 1)
                W.w(v, "v", sems["b2pv"], 4 * CUM[kv + 1] - 4 + qi + 1)
                if kv == 0 and qi == 0:
                    W.w(v, "v", sems["b1mm"], 64)
                v.scalar_tensor_tensor(
                    _sl(arena2, bl * 16384 + qh * 512, 512),
                    ps[qi][:], 1.0, _sl(invb, 0, 512),
                    OP.mult, OP.mult).then_inc(sems["b2stt"], 1)

    B2_DONE = [("b2ex", 1536), ("b2stt", 64), ("b2rc", 16), ("b2bb", 64),
               ("b2pl", 1536), ("b2pv", 1536), ("b2sm", 1536), ("b2sc", 1536)]

    def phB3_g(g):
        for sname, val in B2_DONE:
            W.w(g, "g", sems[sname], val)
        if DEBUG_TAPS:
            W.w(g, "g", sems["stap"], 32)
        for h2 in range(64):
            hc = h2 % 32
            W.w(g, "g", sems["b3mm"], h2 - 1)
            g.dma_start(out=_sl(wo_ch, (h2 % 2) * 4096, 4096),
                        in_=dt["wo_h"][hc]).then_inc(sems["b3w"], 16)

    def phB3_y(y):
        for sname, val in B2_DONE:
            W.w(y, "y", sems[sname], val)
        for bl in range(2):
            for hc in range(32):
                h2 = bl * 32 + hc
                W.w(y, "y", sems["b3rs"], h2 - 1)
                y.dma_start(out=_sl(xf, (h2 % 2) * 512, 512),
                            in_=dt["xownT"][hc * 128:(hc + 1) * 128,
                                            bl * 512:(bl + 1) * 512]
                            ).then_inc(sems["b3x"], 16)
            for hc in range(32):
                h2 = bl * 32 + hc
                W.w(y, "y", sems["b3x1"], h2 + 1)
                y.dma_start(out=x1dram[hc * 128:(hc + 1) * 128,
                                       bl * 512:(bl + 1) * 512],
                            in_=_sl(x1f, (h2 % 2) * 512, 512)
                            ).then_inc(sems["b3xs"], 16)

    def _b3_lnsums(t, bl, hc):
        h2 = bl * 32 + hc
        W.w(t, "t", sems["b3rs"], h2 + 1)
        if hc == 0:
            W.w(t, "t", sems["b3me"], bl)
            W.w(t, "t", sems["b3e2"], bl)
        t.matmul(ps[2][0:1, :], onescol[:], _sl(resid, hc * 512, 512),
                 start=(hc == 0), stop=(hc == 31)).then_inc(sems["b3s1"], 1)
        W.w(t, "t", sems["b3sq"], h2 + 1)
        t.matmul(ps[3][0:1, :], onescol[:], _sl(sqv, (h2 % 2) * 512, 512),
                 start=(hc == 0), stop=(hc == 31)).then_inc(sems["b3s2"], 1)

    def phB3_t(t):
        for sname, val in B2_DONE:
            W.w(t, "t", sems[sname], val)
        for bl in range(2):
            for hc in range(32):
                h2 = bl * 32 + hc
                W.w(t, "t", sems["b3w"], 16 * (h2 + 1))
                W.w(t, "t", sems["b3rs"], h2 - 1)
                for oc in range(32):
                    ins = t.matmul(
                        ps[h2 % 2][:],
                        _sl(wo_ch, (h2 % 2) * 4096 + oc * 128, 128),
                        _sl(arena2, bl * 16384 + oc * 512, 512),
                        start=(oc == 0), stop=(oc == 31))
                ins.then_inc(sems["b3mm"], 1)
                if hc >= 1:
                    _b3_lnsums(t, bl, hc - 1)
            _b3_lnsums(t, bl, 31)
            W.w(t, "t", sems["b3me"], bl + 1)
            W.w(t, "t", sems["b3mbe"], 2 * bl)
            t.matmul(ps[4][:], ones1f[:], mS[:],
                     start=True, stop=True).then_inc(sems["b3bc"], 1)
            W.w(t, "t", sems["b3rstd"], bl + 1)
            t.matmul(ps[5][:], ones1f[:], rstdS[:],
                     start=True, stop=True).then_inc(sems["b3bc"], 1)

    def phB3_s(s):
        for bl in range(2):
            W.w(s, "s", sems["b3s1"], 32 * (bl + 1))
            s.activation(mS[:], ps[2][0:1, :], AF.Identity,
                         scale=1.0 / D).then_inc(sems["b3me"], 1)
            W.w(s, "s", sems["b3s2"], 32 * (bl + 1))
            s.activation(tA5[:], ps[3][0:1, :], AF.Identity,
                         scale=1.0 / D).then_inc(sems["b3e2"], 1)
            W.w(s, "s", sems["b3var"], bl + 1)
            s.activation(tB5[:], tA5[:], AF.Ln, bias=eps1[0:1, :])
            s.activation(rstdS[:], tB5[:], AF.Exp,
                         scale=-0.5).then_inc(sems["b3rstd"], 1)
            W.w(s, "s", sems["b3bc"], 2 * bl + 1)
            s.activation(mb2[:], ps[4][:], AF.Identity).then_inc(sems["b3mbe"], 1)
            W.w(s, "s", sems["b3bc"], 2 * bl + 2)
            s.activation(rb2[:], ps[5][:], AF.Identity).then_inc(sems["b3mbe"], 1)

    def phB3_v(v):
        for bl in range(2):
            for hc in range(32):
                h2 = bl * 32 + hc
                W.w(v, "v", sems["b3mm"], h2 + 1)
                W.w(v, "v", sems["b3x"], 16 * (h2 + 1))
                if bl == 1 and hc == 0:
                    W.w(v, "v", sems["b3x1"], 32)
                v.tensor_add(_sl(resid, hc * 512, 512), ps[h2 % 2][:],
                             _sl(xf, (h2 % 2) * 512, 512)
                             ).then_inc(sems["b3rs"], 1)
                W.w(v, "v", sems["b3s2"], h2 - 1)
                v.tensor_mul(_sl(sqv, (h2 % 2) * 512, 512),
                             _sl(resid, hc * 512, 512),
                             _sl(resid, hc * 512, 512)).then_inc(sems["b3sq"], 1)
            W.w(v, "v", sems["b3me"], bl + 1)
            vsync(v, v.tensor_mul(tB5[:], mS[:], mS[:]))
            W.w(v, "v", sems["b3e2"], bl + 1)
            v.tensor_sub(tA5[:], tA5[:], tB5[:]).then_inc(sems["b3var"], 1)
            W.w(v, "v", sems["b3mbe"], 2 * bl + 2)
            W.w(v, "v", sems["b3mm"], 32 * (bl + 1))
            for hc in range(32):
                h2 = bl * 32 + hc
                v.scalar_tensor_tensor(_sl(t1f, (h2 % 2) * 512, 512),
                                       _sl(resid, hc * 512, 512), 1.0,
                                       mb2[:], OP.mult, OP.subtract)
                W.w(v, "v", sems["b3xs"], 16 * (h2 - 1))
                v.tensor_mul(_sl(x1f, (h2 % 2) * 512, 512),
                             _sl(t1f, (h2 % 2) * 512, 512), rb2[:])
                v.tensor_copy(_sl(arena2, bl * 16384 + hc * 512, 512),
                              _sl(x1f, (h2 % 2) * 512, 512)
                              ).then_inc(sems["b3x1"], 1)

    B3_DONE = [("b3s1", 64), ("b3s2", 64), ("b3xs", 1024), ("b3x1", 64),
               ("b3mbe", 4)]

    def phB4_g(g):
        for sname, val in B3_DONE:
            W.w(g, "g", sems[sname], val)
        g.dma_start(out=bcat[:], in_=dt["bcat_h"][:]).then_inc(sems["s0"], 16)
        g.dma_start(out=acat[:], in_=dt["acat_h"][:]).then_inc(sems["s0"], 16)
        g.dma_start(out=rw[:], in_=dt["rw_h"][:]).then_inc(sems["s0"], 16)
        for bl in range(2):
            for oc in range(32):
                o2 = bl * 32 + oc
                W.w(g, "g", sems["b4mm"], o2 - 1)
                g.dma_start(out=_sl(wb_ch, (o2 % 2) * 4096, 4096),
                            in_=dt["wb_h"][oc]).then_inc(sems["b4wb"], 16)

    def phB4_y(y):
        for sname, val in B3_DONE:
            W.w(y, "y", sems[sname], val)
        y.dma_start(out=bbt[:], in_=dt["bb_h"][:]).then_inc(sems["s0"], 16)
        for bl in range(2):
            for oc in range(32):
                o2 = bl * 32 + oc
                W.w(y, "y", sems["b4r2"], o2 - 1)
                W.w(y, "y", sems["b3xs"], 16 * (bl * 32 + oc + 1))
                y.dma_start(out=_sl(xf1, (o2 % 2) * 512, 512),
                            in_=x1dram[oc * 128:(oc + 1) * 128,
                                       bl * 512:(bl + 1) * 512]
                            ).then_inc(sems["b4x"], 16)
            for hc in range(32):
                h2 = bl * 32 + hc
                W.w(y, "y", sems["b4o"], h2 + 1)
                y.dma_start(out=out_d[hc * 128:(hc + 1) * 128,
                                      bl * 512:(bl + 1) * 512],
                            in_=_sl(outf, (h2 % 2) * 512, 512)
                            ).then_inc(sems["sout"], 16)
        if DEBUG_TAPS:
            y.dma_start(out=taps["tapA"][:, 0:1], in_=mask16[:], allow_slow_non_contiguous=True).then_inc(sems["stap"], 16)
            y.dma_start(out=taps["tapA"][:, 1:2], in_=biasv[:], allow_slow_non_contiguous=True).then_inc(sems["stap"], 16)
            y.dma_start(out=taps["tapA"][:, 2:3], in_=lnp16[:], allow_slow_non_contiguous=True).then_inc(sems["stap"], 16)
            y.dma_start(out=taps["tapA"][:, 3:4], in_=t16a[:], allow_slow_non_contiguous=True).then_inc(sems["stap"], 16)
            y.dma_start(out=taps["tapA"][:, 4:5], in_=t16b[:], allow_slow_non_contiguous=True).then_inc(sems["stap"], 16)
            y.dma_start(out=taps["tapA"][:, 5:9], in_=eq16[:], allow_slow_non_contiguous=True).then_inc(sems["stap"], 16)
            y.dma_start(out=taps["tapA"][0:1, 9:13], in_=act_f[:], allow_slow_non_contiguous=True).then_inc(sems["stap"], 16)
            y.dma_start(out=taps["tapA"][:, 16:528], in_=se[:]).then_inc(sems["stap"], 16)
            y.dma_start(out=taps["tapK"][:], in_=kdram[:]).then_inc(sems["stap"], 16)
            y.dma_start(out=taps["tapV"][:], in_=vdram[:]).then_inc(sems["stap"], 16)
            W.w(y, "y", sems["b3xs"], 1024)
            y.dma_start(out=taps["tapX1"][:], in_=x1dram[:]).then_inc(sems["stap"], 16)
            W.w(y, "y", sems["stap"], 208)
        W.w(y, "y", sems["sout"], 16 * 64)

    def _b4_lnsums(t, bl, oc):
        o2 = bl * 32 + oc
        W.w(t, "t", sems["b4r2"], o2 + 1)
        t.matmul(ps[2][0:1, :], onescol[:], _sl(r2, oc * 512, 512),
                 start=(oc == 0), stop=(oc == 31)).then_inc(sems["b4s1"], 1)
        W.w(t, "t", sems["b4sq"], o2 + 1)
        t.matmul(ps[3][0:1, :], onescol[:], _sl(sq2, (o2 % 2) * 512, 512),
                 start=(oc == 0), stop=(oc == 31)).then_inc(sems["b4s2"], 1)

    def phB4_t(t):
        W.w(t, "t", sems["b3rs"], 64)
        W.w(t, "t", sems["b3me"], 2)
        W.w(t, "t", sems["b3e2"], 2)
        W.w(t, "t", sems["b3mbe"], 4)
        W.w(t, "t", sems["s0"], S0B4)
        for bl in range(2):
            W.w(t, "t", sems["b3x1"], 32 * (bl + 1))
            W.w(t, "t", sems["b4w"], bl)
            W.w(t, "t", sems["b4me"], bl)
            W.w(t, "t", sems["b4e2"], bl)
            for hc in range(HC):
                ins = t.matmul(ps[2][0:16, :], rw[:, hc * 16:hc * 16 + 16],
                               _sl(arena2, bl * 16384 + hc * 512, 512),
                               start=(hc == 0), stop=(hc == HC - 1))
            ins.then_inc(sems["b4r"], 1)
            W.w(t, "t", sems["b4ze"], bl)
            for hc in range(HC):
                ins = t.matmul(ps[3][:], _sl(acat, hc * 128, 128),
                               _sl(arena2, bl * 16384 + hc * 512, 512),
                               start=(hc == 0), stop=(hc == HC - 1))
            ins.then_inc(sems["b4z"], 1)
            W.w(t, "t", sems["b4se"], bl + 1)
            W.w(t, "t", sems["b4iv"], bl)
            t.matmul(ps[4][0:1, :], ones16[:], se[:],
                     start=True, stop=True).then_inc(sems["b4sm"], 1)
            W.w(t, "t", sems["b4iv"], bl + 1)
            W.w(t, "t", sems["b4al"], bl)
            t.matmul(ps[5][0:16, :], ones1f[:, 0:16], inv1[:],
                     start=True, stop=True).then_inc(sems["b4b16"], 1)
            W.w(t, "t", sems["b4al"], bl + 1)
            t.matmul(ps[6][:], indm[:], alpha[:],
                     start=True, stop=True).then_inc(sems["b4aem"], 1)
            for oc in range(32):
                o2 = bl * 32 + oc
                W.w(t, "t", sems["b4wb"], 16 * (o2 + 1))
                W.w(t, "t", sems["b4ye"], o2 - 1)
                if oc == 0:
                    W.w(t, "t", sems["b4w"], bl + 1)
                for hc in range(HC):
                    t.matmul(ps[o2 % 2][:],
                             _sl(wb_ch, (o2 % 2) * 4096 + hc * 128, 128),
                             _sl(arena2, bl * 16384 + hc * 512, 512),
                             start=(hc == 0), stop=False)
                t.matmul(ps[o2 % 2][:], _sl(bcat, oc * 128, 128),
                         _sl(wbf, (bl % 2) * 512, 512),
                         start=False, stop=True).then_inc(sems["b4mm"], 1)
                if oc >= 1:
                    _b4_lnsums(t, bl, oc - 1)
            _b4_lnsums(t, bl, 31)
            W.w(t, "t", sems["b4me"], bl + 1)
            W.w(t, "t", sems["b4mbe"], 2 * bl)
            W.w(t, "t", sems["b4iv"], bl + 1)
            t.matmul(ps[4][:], ones1f[:], mS[:],
                     start=True, stop=True).then_inc(sems["b4bc"], 1)
            W.w(t, "t", sems["b4rstd"], bl + 1)
            t.matmul(ps[5][:], ones1f[:], rstdS[:],
                     start=True, stop=True).then_inc(sems["b4bc"], 1)

    def phB4_s(s):
        W.w(s, "s", sems["sp_v"], 11)
        for bl in range(2):
            W.w(s, "s", sems["b4r"], bl + 1)
            s.activation(se[:], ps[2][0:16, :], AF.Exp,
                         bias=biasv[:]).then_inc(sems["b4se"], 1)
            W.w(s, "s", sems["b4z"], bl + 1)
            s.activation(zsb[:], ps[3][:], AF.Identity).then_inc(sems["b4ze"], 1)
            for oc in range(32):
                o2 = bl * 32 + oc
                W.w(s, "s", sems["b4mm"], o2 + 1)
                W.w(s, "s", sems["b4r2"], o2 - 1)
                s.activation(_sl(fY, (o2 % 2) * 512, 512), ps[o2 % 2][:],
                             AF.Identity, bias=bbt[:, oc:oc + 1]
                             ).then_inc(sems["b4ye"], 1)
            W.w(s, "s", sems["b4s1"], 32 * (bl + 1))
            s.activation(mS[:], ps[2][0:1, :], AF.Identity,
                         scale=1.0 / D).then_inc(sems["b4me"], 1)
            W.w(s, "s", sems["b4s2"], 32 * (bl + 1))
            s.activation(tA5[:], ps[3][0:1, :], AF.Identity,
                         scale=1.0 / D).then_inc(sems["b4e2"], 1)
            W.w(s, "s", sems["b4var"], bl + 1)
            s.activation(tB5[:], tA5[:], AF.Ln, bias=eps1[0:1, :])
            s.activation(rstdS[:], tB5[:], AF.Exp,
                         scale=-0.5).then_inc(sems["b4rstd"], 1)
            W.w(s, "s", sems["b4bc"], 2 * bl + 1)
            s.activation(mb4[:], ps[4][:], AF.Identity).then_inc(sems["b4mbe"], 1)
            W.w(s, "s", sems["b4bc"], 2 * bl + 2)
            s.activation(rb4[:], ps[5][:], AF.Identity).then_inc(sems["b4mbe"], 1)

    def phB4_v(v):
        for bl in range(2):
            W.w(v, "v", sems["b4sm"], bl + 1)
            W.w(v, "v", sems["b4b16"], bl)
            v.reciprocal(inv1[:], ps[4][0:1, :]).then_inc(sems["b4iv"], 1)
            W.w(v, "v", sems["b4b16"], bl + 1)
            W.w(v, "v", sems["b4aem"], bl)
            v.tensor_mul(alpha[:], se[:], ps[5][0:16, :]).then_inc(sems["b4al"], 1)
            W.w(v, "v", sems["b4aem"], bl + 1)
            W.w(v, "v", sems["b4ze"], bl + 1)
            v.scalar_tensor_tensor(_sl(wbf, (bl % 2) * 512, 512),
                                   zsb[:], LORA_SCALING, ps[6][:],
                                   OP.mult, OP.mult).then_inc(sems["b4w"], 1)
            for oc in range(32):
                o2 = bl * 32 + oc
                W.w(v, "v", sems["b4ye"], o2 + 1)
                W.w(v, "v", sems["b4x"], 16 * (o2 + 1))
                if bl == 1 and oc == 0:
                    W.w(v, "v", sems["b4o"], 32)
                v.tensor_add(_sl(r2, oc * 512, 512),
                             _sl(fY, (o2 % 2) * 512, 512),
                             _sl(xf1, (o2 % 2) * 512, 512)
                             ).then_inc(sems["b4r2"], 1)
                W.w(v, "v", sems["b4s2"], o2 - 1)
                v.tensor_mul(_sl(sq2, (o2 % 2) * 512, 512),
                             _sl(r2, oc * 512, 512),
                             _sl(r2, oc * 512, 512)).then_inc(sems["b4sq"], 1)
            W.w(v, "v", sems["b4me"], bl + 1)
            vsync(v, v.tensor_mul(tB5[:], mS[:], mS[:]))
            W.w(v, "v", sems["b4e2"], bl + 1)
            v.tensor_sub(tA5[:], tA5[:], tB5[:]).then_inc(sems["b4var"], 1)
            W.w(v, "v", sems["b4mbe"], 2 * bl + 2)
            for hc in range(32):
                h2 = bl * 32 + hc
                v.scalar_tensor_tensor(_sl(t1g, (h2 % 2) * 512, 512),
                                       _sl(r2, hc * 512, 512), 1.0,
                                       mb4[:], OP.mult, OP.subtract)
                W.w(v, "v", sems["sout"], 16 * (h2 - 1))
                v.tensor_mul(_sl(outf, (h2 % 2) * 512, 512),
                             _sl(t1g, (h2 % 2) * 512, 512),
                             rb4[:]).then_inc(sems["b4o"], 1)

    with nc.Block() as blk:

        @blk.sync
        def _(y):
            ph0_sync(y)
            phA1_y(y)
            phA2_y(y)
            phB1_y(y)
            phB2_y(y)
            phB3_y(y)
            phB4_y(y)

        @blk.gpsimd
        def _(g):
            phA1_g(g)
            phA2_g(g)
            phB1_g(g)
            phB2_g(g)
            phB3_g(g)
            phB4_g(g)

        @blk.tensor
        def _(t):
            phP_t(t)
            phA1_t(t)
            phA2_t(t)
            phB1_t(t)
            phB2_t(t)
            phB3_t(t)
            phB4_t(t)

        @blk.scalar
        def _(s):
            phP_s(s)
            phA1_s(s)
            phA2_s(s)
            phB1_s(s)
            phB2_s(s)
            phB3_s(s)
            phB4_s(s)

        @blk.vector
        def _(v):
            phP_v(v)
            phA1_v(v)
            phB1_v(v)
            phB2_v(v)
            phB3_v(v)
            phB4_v(v)

    return nc, taps


_CACHE = {}


def kernel(**inputs):
    maps = _prep_in_maps(inputs)
    if "nc" not in _CACHE:
        _CACHE["nc"] = build_program()
    nc, taps = _CACHE["nc"]
    res = run_bass_kernel_spmd(nc, maps, core_ids=list(range(8)))
    y = np.empty((2, S, D), np.float32)
    for c in range(8):
        b, q0a, q0b = _core_blocks(c)
        sh = res.results[c]["out"]
        y[b, q0a:q0a + 512, :] = sh[:, 0:512].T
        y[b, q0b:q0b + 512, :] = sh[:, 512:1024].T
    if DEBUG_TAPS:
        kernel.last_results = res
    return y


# version 07 (final)
# speedup vs baseline: 1.0075x; 1.0075x over previous
import os
import numpy as np
import ml_dtypes
import concourse.bass as bass
import concourse.mybir as mybir
from concourse.bass_utils import run_bass_kernel_spmd

F32 = mybir.dt.float32
BF16 = mybir.dt.bfloat16
I32 = mybir.dt.int32
AF = mybir.ActivationFunctionType
AX = mybir.AxisListType
OP = mybir.AluOpType
BF = ml_dtypes.bfloat16

S = 4096
D = 4096
HC = 32
NTB = 8
KOC = 8
KVH = 8
NCH = 32
SCALE = 1.0 / float(np.sqrt(128.0))
CSHIFT = -8.0
EPS_LN = 1e-5
EPS_MOE = 1e-12
LORA_SCALING = 16.0 / 8.0

DEBUG_TAPS = os.environ.get("MK_DEBUG_TAPS", "0") == "1"



def _tile_w(w):
    O, H = w.shape
    return np.ascontiguousarray(
        w.reshape(O // 128, 128, H // 128, 128).transpose(0, 3, 2, 1))


def _consts():
    pos = np.arange(S, dtype=np.float32)[:, None]
    freqs = np.exp(-np.arange(0, 128, 2, dtype=np.float32)
                   * (np.log(10000.0) / 128.0))[None, :]
    ang = pos * freqs
    cos = np.concatenate([np.cos(ang), np.cos(ang)], axis=1).T
    sin = np.concatenate([np.sin(ang), np.sin(ang)], axis=1).T
    rot = np.zeros((128, 128), np.float32)
    for dp in range(64):
        rot[dp + 64, dp] = -1.0
        rot[dp, dp + 64] = 1.0
    ones4v = np.zeros((128, 16), np.float32)
    for qi in range(4):
        ones4v[:, qi * 4 + qi] = 1.0
    ebc = np.zeros((4, 512), np.float32)
    for qi in range(4):
        ebc[qi, qi * 128:(qi + 1) * 128] = 1.0
    indm = np.zeros((16, 128), np.float32)
    for e in range(16):
        indm[e, e * 8:(e + 1) * 8] = 1.0
    return {
        "cosK": np.ascontiguousarray(cos).astype(BF),
        "sinK": np.ascontiguousarray(sin).astype(BF),
        "rotR": rot.astype(BF), "ones4v": ones4v.astype(BF),
        "onescol_f": np.ones((128, 1), np.float32),
        "ones1_f": np.ones((1, 128), np.float32),
        "ones16": np.ones((16, 1), np.float32),
        "ebc": ebc, "indm": indm,
        "iota16": np.arange(16, dtype=np.float32)[:, None],
    }


def _core_blocks(c):
    j = c % 4
    return c // 4, 512 * j, 512 * (7 - j)


def _masks_for(q0s):
    qt = np.arange(512)[None, :]
    out = np.empty((64, 128, 512), np.float32)
    for bl, q0 in enumerate(q0s):
        for ch in range(NCH):
            kt = np.arange(128)[:, None] + ch * 128
            out[bl * 32 + ch] = (q0 + qt) >= kt
    return out.reshape(64 * 128, 512).astype(BF)


def _prep_in_maps(inp):
    x = np.asarray(inp["L0_attn_hidden_states"], np.float32)
    con = _consts()
    xT = [np.ascontiguousarray(x[b].T) for b in range(2)]
    shared = {
        "wq_h": _tile_w(np.asarray(inp["L0_attn_Wq"], np.float32)),
        "wb_h": _tile_w(np.asarray(inp["L0_ffn_W_base"], np.float32)),
        "wo_h": np.ascontiguousarray(
            np.asarray(inp["L0_attn_Wo"], np.float32)
            .reshape(32, 128, 32, 128).transpose(0, 3, 2, 1)),
        "wk_h": np.ascontiguousarray(
            np.asarray(inp["L0_attn_Wk"], np.float32).reshape(8, 128, 32, 128)
            .transpose(3, 2, 0, 1).reshape(128, 32, 1024)),
        "wv_h": np.ascontiguousarray(
            np.asarray(inp["L0_attn_Wv"], np.float32).reshape(8, 128, 32, 128)
            .transpose(3, 2, 0, 1).reshape(128, 32, 1024)),
        "rw_h": np.ascontiguousarray(
            np.asarray(inp["L0_ffn_router_W"], np.float32).reshape(16, 32, 128)
            .transpose(2, 1, 0)),
        "acat_h": np.ascontiguousarray(
            np.asarray(inp["L0_ffn_lora_A"], np.float32).reshape(128, 32, 128)
            .transpose(2, 1, 0)),
        "bcat_h": np.ascontiguousarray(
            np.asarray(inp["L0_ffn_lora_B"], np.float32).transpose(0, 2, 1)
            .reshape(128, 4096)),
        "bb_h": np.ascontiguousarray(
            np.asarray(inp["L0_ffn_b_base"], np.float32).reshape(32, 128).T),
        "rb_h": np.asarray(inp["L0_ffn_router_b"], np.float32),
        "prior_h": np.asarray(inp["L0_ffn_domain_prior"], np.float32),
        "act_h": np.asarray(inp["L0_ffn_active"], np.int32),
    }
    shared.update(con)
    maps = []
    for c in range(8):
        b, q0a, q0b = _core_blocks(c)
        m = dict(shared)
        m["xT"] = xT[b]
        m["xownT"] = np.ascontiguousarray(np.concatenate(
            [xT[b][:, q0a:q0a + 512], xT[b][:, q0b:q0b + 512]], axis=1))
        m["cosQ"] = np.ascontiguousarray(np.concatenate(
            [con["cosK"][:, q0a:q0a + 512], con["cosK"][:, q0b:q0b + 512]], axis=1))
        m["sinQ"] = np.ascontiguousarray(np.concatenate(
            [con["sinK"][:, q0a:q0a + 512], con["sinK"][:, q0b:q0b + 512]], axis=1))
        m["masks"] = _masks_for((q0a, q0b))
        maps.append(m)
    return maps



class _Waits:
    def __init__(self):
        self.seen = {}

    def w(self, eng, ename, sem, val):
        if val <= 0:
            return
        key = (ename, sem.name)
        if self.seen.get(key, 0) >= val:
            return
        eng.wait_ge(sem, val)
        self.seen[key] = val


def _sl(tensor, off, n):
    return tensor[:, off:off + n]


def build_program():
    nc = bass.Bass("TRN2", target_bir_lowering=False)

    dt = {}

    def din(name, shape, dty=F32):
        dt[name] = nc.dram_tensor(name, list(shape), dty, kind="ExternalInput")

    din("xT", (D, S)); din("xownT", (D, 1024))
    din("wq_h", (32, 128, 32, 128)); din("wb_h", (32, 128, 32, 128))
    din("wo_h", (32, 128, 32, 128))
    din("wk_h", (128, 32, 1024)); din("wv_h", (128, 32, 1024))
    din("rw_h", (128, 32, 16)); din("acat_h", (128, 32, 128))
    din("bcat_h", (128, 4096)); din("bb_h", (128, 32))
    din("rb_h", (16,)); din("prior_h", (16,)); din("act_h", (4,), I32)
    din("cosK", (128, S), BF16); din("sinK", (128, S), BF16)
    din("cosQ", (128, 1024), BF16); din("sinQ", (128, 1024), BF16)
    din("rotR", (128, 128), BF16); din("ones4v", (128, 16), BF16)
    din("onescol_f", (128, 1)); din("ones1_f", (1, 128)); din("ones16", (16, 1))
    din("ebc", (4, 512)); din("indm", (16, 128)); din("iota16", (16, 1))
    din("masks", (64 * 128, 512), BF16)

    out_d = nc.dram_tensor("out", [D, 1024], F32, kind="ExternalOutput")
    kdram = nc.dram_tensor("kdram", [1024, S], BF16, kind="Internal")
    vdram = nc.dram_tensor("vdram", [KVH, 128, 32, 128], BF16, kind="Internal")
    x1dram = nc.dram_tensor("x1dram", [D, 1024], F32, kind="Internal")
    taps = {}
    if DEBUG_TAPS:
        taps["tapK"] = nc.dram_tensor("tapK", [1024, S], BF16, kind="ExternalOutput")
        taps["tapV"] = nc.dram_tensor("tapV", [KVH, 128, 32, 128], BF16,
                                      kind="ExternalOutput")
        taps["tapQ"] = nc.dram_tensor("tapQ", [128, 32768], BF16,
                                      kind="ExternalOutput")
        taps["tapOT"] = nc.dram_tensor("tapOT", [128, 32768], BF16,
                                       kind="ExternalOutput")
        taps["tapX1"] = nc.dram_tensor("tapX1", [D, 1024], F32,
                                       kind="ExternalOutput")
        taps["tapA"] = nc.dram_tensor("tapA", [16, 528], F32,
                                      kind="ExternalOutput")

    sems = {}
    sem_names = [
        "s0", "sp_v", "sp_s", "sp_t",
        "a1w", "a1x", "a1mm", "a1kraw", "a1rot", "a1tt1", "a1cmb", "a1kst",
        "a2w", "a2x", "a2mm", "a2ev", "a2vst",
        "b1x", "b1w", "b1mm", "b1qraw", "b1rot", "b1tt1", "b1cmb",
        "b2k0", "b2k1", "b2v0", "b2v1", "b2m0", "b2m1", "b2sc", "b2ex", "b2pl", "b2pv", "b2sm",
        "b2rc", "b2bc", "b2bb", "b2stt",
        "b3w", "b3x", "b3mm", "b3rs", "b3sq", "b3s1", "b3s2",
        "b3me", "b3e2", "b3var", "b3rstd", "b3bc", "b3mbe", "b3x1", "b3xs",
        "b4wb", "b4x", "b4mm", "b4ye", "b4r2", "b4sq", "b4s1", "b4s2",
        "b4r", "b4z", "b4se", "b4sm", "b4iv", "b4b16", "b4al", "b4aem",
        "b4ze", "b4w", "b4me", "b4e2", "b4var", "b4rstd", "b4bc", "b4mbe",
        "b4o", "sout", "stap", "vch",
    ]
    for name in sem_names:
        cm = nc.semaphore(name)
        sems[name] = cm.__enter__()

    nc.alloc_sbuf_tensor("mk_reserved", [128, 211872], mybir.dt.dt.uint8
                         if hasattr(mybir.dt, "dt") else mybir.dt.uint8)
    cursor = [16512]

    def at(name, shape, dty, off=None):
        if off is None:
            off = cursor[0]
        nbytes = int(np.prod(shape[1:])) * {F32: 4, BF16: 2, I32: 4}[dty]
        h = nc.alloc_sbuf_tensor_at(name, list(shape), dty, offset=off)
        cursor[0] = off + ((nbytes + 31) // 32) * 32
        return h

    rotR = at("rotR_sb", [128, 128], BF16)
    ones4v = at("ones4v_sb", [128, 16], BF16)
    onescol = at("onescol_sb", [128, 1], F32)
    ones1f = at("ones1f_sb", [1, 128], F32)
    ones16 = at("ones16_sb", [16, 1], F32)
    ebc = at("ebc_sb", [4, 512], F32)
    indm = at("indm_sb", [16, 128], F32)
    iota16 = at("iota16_sb", [16, 1], F32)
    negC = at("negC_sb", [128, 1], F32)
    eps16 = at("eps16_sb", [16, 1], F32)
    eps1 = at("eps1_sb", [1, 1], F32)
    rb16 = at("rb16_sb", [16, 1], F32)
    prior16 = at("prior16_sb", [16, 1], F32)
    act_i = at("act_i_sb", [1, 4], I32)
    act_f = at("act_f_sb", [1, 4], F32)
    eq16 = at("eq16_sb", [16, 4], F32)
    mask16 = at("mask16_sb", [16, 1], F32)
    lnp16 = at("lnp16_sb", [16, 1], F32)
    t16a = at("t16a_sb", [16, 1], F32)
    t16b = at("t16b_sb", [16, 1], F32)
    biasv = at("biasv_sb", [16, 1], F32)
    mS = at("m_sb", [1, 512], F32)
    rstdS = at("rstd_sb", [1, 512], F32)
    tA5 = at("tA5_sb", [1, 512], F32)
    tB5 = at("tB5_sb", [1, 512], F32)
    arena2 = at("arena2", [128, 32768], BF16)
    PHASE0 = cursor[0]

    wkv = at("wkv_sb", [128, HC * 1024], BF16, PHASE0)
    kvout = at("kvout_sb", [128, 2 * 4096], BF16)
    krawA = at("krawA", [128, 2 * 512], BF16)
    c1A = at("c1A", [128, 2 * 512], F32)
    c2A = at("c2A", [128, 2 * 512], F32)
    cosK = at("cosK_sb", [128, S], BF16)
    sinK = at("sinK_sb", [128, S], BF16)

    qT = at("qT_sb", [128, 32 * 1024], BF16, PHASE0)
    PH_B1 = PHASE0 + 65536
    cosQ = at("cosQ_sb", [128, 1024], BF16, PH_B1)
    sinQ = at("sinQ_sb", [128, 1024], BF16)
    wq_ch = at("wq_ch", [128, 2 * 4096], BF16)
    krawB = at("krawB", [128, 2 * 512], BF16)
    c1B = at("c1B", [128, 2 * 512], F32)
    c2B = at("c2B", [128, 2 * 512], F32)

    Kb = at("Kb_sb", [128, 2 * 4096], BF16, PH_B1)
    Vb = at("Vb_sb", [128, 2 * 4096], BF16)
    Pt = at("Pt_sb", [128, 4 * 512], BF16)
    Pm = at("Pm_sb", [128, 4 * 512], BF16)
    mkb = at("mkb_sb", [128, 2 * 512], BF16)
    inv4 = at("inv4_sb", [4, 512], F32)
    invb = at("invb_sb", [128, 2 * 512], F32)

    resid = at("resid_sb", [128, 32 * 512], F32, PHASE0)
    wo_ch = at("wo_ch", [128, 2 * 4096], BF16, PH_B1)
    xf = at("xf_sb", [128, 2 * 512], F32)
    t1f = at("t1f_sb", [128, 2 * 512], F32)
    x1f = at("x1f_sb", [128, 2 * 512], F32)
    sqv = at("sq_sb", [128, 2 * 512], F32)
    mb2 = at("mb2_sb", [128, 512], F32)
    rb2 = at("rb2_sb", [128, 512], F32)

    r2 = at("r2_sb", [128, 32 * 512], F32, PHASE0)
    wb_ch = at("wb_ch", [128, 2 * 4096], BF16, PH_B1)
    bcat = at("bcat_sb", [128, 4096], BF16)
    acat = at("acat_sb", [128, 4096], BF16)
    rw = at("rw_sb", [128, 512], BF16)
    bbt = at("bbt_sb", [128, 32], F32)
    xf1 = at("xf1_sb", [128, 2 * 512], F32)
    fY = at("fY_sb", [128, 2 * 512], F32)
    sq2 = at("sq2_sb", [128, 2 * 512], F32)
    outf = at("outf_sb", [128, 2 * 512], F32)
    t1g = at("t1g_sb", [128, 2 * 512], F32)
    wbf = at("wbf_sb", [128, 2 * 512], BF16)
    zsb = at("z_sb", [128, 512], F32)
    se = at("se_sb", [16, 512], F32)
    alpha = at("alpha_sb", [16, 512], F32)
    inv1 = at("inv1_sb", [1, 512], F32)
    mb4 = at("mb4_sb", [128, 512], F32)
    rb4 = at("rb4_sb", [128, 512], F32)
    assert cursor[0] <= 229344, f"SBUF over budget: {cursor[0]}"
    print("MK sbuf map end:", cursor[0])

    ps = []
    for i in range(8):
        cm = nc.psum_tensor(f"ps{i}", [128, 512], F32)
        ps.append(cm.__enter__())

    W = _Waits()
    NCONST = 13
    S0F = 16 * NCONST
    S0B1 = S0F + 32
    S0B4 = S0B1 + 64

    def xa_sl(tb, hc, lo=0, n=512):
        base = (tb % 2) * 16384 + hc * 512 + lo
        return arena2[:, base:base + n]

    def ph0_sync(y):
        loads = [
            (cosK, dt["cosK"][:]), (sinK, dt["sinK"][:]),
            (rotR, dt["rotR"][:]), (ones4v, dt["ones4v"][:]),
            (onescol, dt["onescol_f"][:]), (ones1f, dt["ones1_f"][:]),
            (ones16, dt["ones16"][:]), (ebc, dt["ebc"][:]),
            (indm, dt["indm"][:]), (iota16, dt["iota16"][:]),
            (rb16, dt["rb_h"][:, None]), (prior16, dt["prior_h"][:, None]),
            (act_i, dt["act_h"][None, :]),
        ]
        assert len(loads) == NCONST
        for sb, ap in loads:
            y.dma_start(out=sb[:], in_=ap).then_inc(sems["s0"], 16)

    vcnt = [0]

    def vsync(v, ins):
        vcnt[0] += 1
        ins.then_inc(sems["vch"], 1)
        v.wait_ge(sems["vch"], vcnt[0])

    def phP_v(v):
        v.memset(negC[:], CSHIFT).then_inc(sems["sp_v"], 1)
        v.memset(eps16[:], EPS_MOE).then_inc(sems["sp_v"], 1)
        v.memset(eps1[:], EPS_LN).then_inc(sems["sp_v"], 1)
        W.w(v, "v", sems["s0"], S0F)
        v.tensor_copy(act_f[:], act_i[:]).then_inc(sems["sp_v"], 1)
        W.w(v, "v", sems["sp_t"], 1)
        vsync(v, v.tensor_single_scalar(eq16[:], ps[0][0:16, 0:4], iota16[:],
                                        OP.is_equal))
        v.sem_inc(sems["sp_v"], 1)
        vsync(v, v.tensor_add(t16a[:], eq16[:, 0:1], eq16[:, 1:2]))
        v.sem_inc(sems["sp_v"], 1)
        vsync(v, v.tensor_add(t16b[:], eq16[:, 2:3], eq16[:, 3:4]))
        v.sem_inc(sems["sp_v"], 1)
        vsync(v, v.tensor_add(mask16[:], t16a[:], t16b[:]))
        v.sem_inc(sems["sp_v"], 1)
        vsync(v, v.tensor_scalar(t16a[:], mask16[:], 1.0, 1e9, OP.subtract,
                                 OP.mult))
        v.sem_inc(sems["sp_v"], 1)
        W.w(v, "v", sems["sp_s"], 1)
        vsync(v, v.tensor_add(t16b[:], lnp16[:], rb16[:]))
        v.sem_inc(sems["sp_v"], 1)
        vsync(v, v.tensor_add(biasv[:], t16a[:], t16b[:]))
        v.sem_inc(sems["sp_v"], 1)

    def phP_s(s):
        W.w(s, "s", sems["s0"], S0F)
        W.w(s, "s", sems["sp_v"], 2)
        s.activation(lnp16[:], prior16[:], AF.Ln,
                     bias=eps16[:]).then_inc(sems["sp_s"], 1)

    def phP_t(t):
        W.w(t, "t", sems["s0"], S0F)
        W.w(t, "t", sems["sp_v"], 4)
        t.matmul(ps[0][0:16, 0:4], ones1f[:, 0:16], act_f[:],
                 start=True, stop=True).then_inc(sems["sp_t"], 1)

    def phA1_g(g):
        g.dma_start(out=wkv[:], in_=dt["wk_h"][:]).then_inc(sems["a1w"], 16)
        for tb in range(NTB):
            W.w(g, "g", sems["a1mm"], 8 * (tb - 1))
            g.dma_start(
                out=_sl(arena2, (tb % 2) * 16384, 16384),
                in_=dt["xT"][:, tb * 512:(tb + 1) * 512]
                .rearrange("(c p) t -> p c t", p=128),
            ).then_inc(sems["a1x"], 16)

    def phA1_t(t):
        W.w(t, "t", sems["sp_v"], 5)
        W.w(t, "t", sems["a1w"], 16)

        def rot(gk):
            W.w(t, "t", sems["a1kraw"], gk + 1)
            W.w(t, "t", sems["a1cmb"], gk - 1)
            t.matmul(ps[2 + gk % 2][:], rotR[:],
                     _sl(krawA, (gk % 2) * 512, 512),
                     start=True, stop=True).then_inc(sems["a1rot"], 1)

        for tb in range(NTB):
            W.w(t, "t", sems["a1x"], 16 * (tb + 1))
            for koc in range(KOC):
                gk = tb * 8 + koc
                W.w(t, "t", sems["a1kraw"], gk - 1)
                for hc in range(HC):
                    ins = t.matmul(
                        ps[gk % 2][:],
                        _sl(wkv, hc * 1024 + koc * 128, 128),
                        xa_sl(tb, hc),
                        start=(hc == 0), stop=(hc == HC - 1))
                ins.then_inc(sems["a1mm"], 1)
                if gk >= 1:
                    rot(gk - 1)
        rot(NTB * KOC - 1)

    def phA1_s(s):
        for gk in range(NTB * KOC):
            W.w(s, "s", sems["a1mm"], gk + 1)
            W.w(s, "s", sems["a1rot"], gk - 1)
            W.w(s, "s", sems["a1tt1"], gk - 1)
            s.activation(_sl(krawA, (gk % 2) * 512, 512),
                         ps[gk % 2][:], AF.Identity).then_inc(sems["a1kraw"], 1)

    def phA1_v(v):
        for gk in range(NTB * KOC):
            tb, koc = gk // 8, gk % 8
            b = gk % 2
            W.w(v, "v", sems["a1kraw"], gk + 1)
            v.tensor_mul(_sl(c1A, b * 512, 512), _sl(krawA, b * 512, 512),
                         cosK[:, tb * 512:tb * 512 + 512]
                         ).then_inc(sems["a1tt1"], 1)
            W.w(v, "v", sems["a1rot"], gk + 1)
            v.scalar_tensor_tensor(_sl(c2A, b * 512, 512), ps[2 + b][:], 1.0,
                                   sinK[:, tb * 512:tb * 512 + 512],
                                   OP.mult, OP.mult)
            if koc == 0:
                W.w(v, "v", sems["a1kst"], 16 * (tb - 1))
            v.tensor_add(_sl(kvout, (tb % 2) * 4096 + koc * 512, 512),
                         _sl(c1A, b * 512, 512),
                         _sl(c2A, b * 512, 512)).then_inc(sems["a1cmb"], 1)

    def phA1_y(y):
        for tb in range(NTB):
            W.w(y, "y", sems["a1cmb"], 8 * (tb + 1))
            y.dma_start(
                out=kdram[:, tb * 512:(tb + 1) * 512]
                .rearrange("(c p) t -> p c t", p=128),
                in_=_sl(kvout, (tb % 2) * 4096, 4096),
            ).then_inc(sems["a1kst"], 16)

    def phA2_g(g):
        W.w(g, "g", sems["a1mm"], 64)
        g.dma_start(out=wkv[:], in_=dt["wv_h"][:]).then_inc(sems["a2w"], 16)
        for tb in range(NTB):
            W.w(g, "g", sems["a2mm"], 8 * (tb - 1))
            g.dma_start(
                out=_sl(arena2, (tb % 2) * 16384, 16384),
                in_=dt["xT"][:, tb * 512:(tb + 1) * 512]
                .rearrange("(c p) t -> p c t", p=128),
            ).then_inc(sems["a2x"], 16)

    def phA2_t(t):
        W.w(t, "t", sems["a1kraw"], 64)
        W.w(t, "t", sems["a2w"], 16)
        for tb in range(NTB):
            W.w(t, "t", sems["a2x"], 16 * (tb + 1))
            for u in range(8):
                tc, dvb = u // 2, u % 2
                gk = tb * 8 + u
                W.w(t, "t", sems["a2ev"], gk - 1)
                for hc in range(HC):
                    ins = t.matmul(
                        ps[gk % 2][:],
                        xa_sl(tb, hc, tc * 128, 128),
                        _sl(wkv, hc * 1024 + dvb * 512, 512),
                        start=(hc == 0), stop=(hc == HC - 1))
                ins.then_inc(sems["a2mm"], 1)

    def phA2_s(s):
        vview = kvout[:].rearrange("p (w k tc d) -> p w k tc d", w=2, k=8, tc=4)
        for gk in range(NTB * 8):
            tb, u = gk // 8, gk % 8
            tc, dvb = u // 2, u % 2
            W.w(s, "s", sems["a2mm"], gk + 1)
            if u == 0:
                W.w(s, "s", sems["a1kst"], 128)
                W.w(s, "s", sems["a2vst"], 16 * 8 * (tb - 1))
            s.activation(
                vview[:, tb % 2:tb % 2 + 1, dvb * 4:dvb * 4 + 4, tc:tc + 1, :],
                ps[gk % 2][:], AF.Identity).then_inc(sems["a2ev"], 1)

    def phA2_y(y):
        vview = kvout[:].rearrange("p (w k tc d) -> p w k tc d", w=2, k=8, tc=4)
        for tb in range(NTB):
            W.w(y, "y", sems["a2ev"], 8 * (tb + 1))
            for kvh in range(KVH):
                y.dma_start(
                    out=vdram[kvh:kvh + 1, :, tb * 4:(tb + 1) * 4, :],
                    in_=vview[:, tb % 2:tb % 2 + 1, kvh:kvh + 1, :, :],
                ).then_inc(sems["a2vst"], 16)

    A_DONE = [("a2mm", 64), ("a1cmb", 64), ("a1kst", 128), ("a2vst", 1024),
              ("a2ev", 64)]

    def phB1_y(y):
        for sname, val in A_DONE:
            W.w(y, "y", sems[sname], val)
        y.dma_start(out=cosQ[:], in_=dt["cosQ"][:]).then_inc(sems["s0"], 16)
        y.dma_start(out=sinQ[:], in_=dt["sinQ"][:]).then_inc(sems["s0"], 16)

    def phB1_g(g):
        for sname, val in A_DONE:
            W.w(g, "g", sems[sname], val)
        for i in range(4):
            g.dma_start(
                out=_sl(arena2, i * 8192, 8192),
                in_=dt["xownT"][i * 1024:(i + 1) * 1024, :]
                .rearrange("(c p) t -> p c t", p=128),
            ).then_inc(sems["b1x"], 16)
        for oc in range(32):
            W.w(g, "g", sems["b1mm"], 2 * (oc - 1))
            g.dma_start(out=_sl(wq_ch, (oc % 2) * 4096, 4096),
                        in_=dt["wq_h"][oc]).then_inc(sems["b1w"], 16)

    def phB1_t(t):
        W.w(t, "t", sems["a2ev"], 64)
        W.w(t, "t", sems["a1cmb"], 64)
        W.w(t, "t", sems["b1x"], 64)

        def rot(u):
            W.w(t, "t", sems["b1qraw"], u + 1)
            W.w(t, "t", sems["b1cmb"], u - 1)
            t.matmul(ps[2 + u % 2][:], rotR[:], _sl(krawB, (u % 2) * 512, 512),
                     start=True, stop=True).then_inc(sems["b1rot"], 1)

        for oc in range(32):
            W.w(t, "t", sems["b1w"], 16 * (oc + 1))
            for tt in range(2):
                u = oc * 2 + tt
                W.w(t, "t", sems["b1qraw"], u - 1)
                for hc in range(HC):
                    ins = t.matmul(
                        ps[u % 2][:],
                        _sl(wq_ch, (oc % 2) * 4096 + hc * 128, 128),
                        _sl(arena2, hc * 1024 + tt * 512, 512),
                        start=(hc == 0), stop=(hc == HC - 1))
                ins.then_inc(sems["b1mm"], 1)
                if u >= 1:
                    rot(u - 1)
        rot(63)

    def phB1_s(s):
        for u in range(64):
            W.w(s, "s", sems["b1mm"], u + 1)
            W.w(s, "s", sems["b1rot"], u - 1)
            W.w(s, "s", sems["b1tt1"], u - 1)
            s.activation(_sl(krawB, (u % 2) * 512, 512),
                         ps[u % 2][:], AF.Identity).then_inc(sems["b1qraw"], 1)

    def phB1_v(v):
        W.w(v, "v", sems["s0"], S0B1)
        W.w(v, "v", sems["a2vst"], 1024)
        W.w(v, "v", sems["a1kst"], 128)
        for u in range(64):
            oc, tt = u // 2, u % 2
            b = u % 2
            W.w(v, "v", sems["b1qraw"], u + 1)
            v.tensor_mul(_sl(c1B, b * 512, 512), _sl(krawB, b * 512, 512),
                         _sl(cosQ, tt * 512, 512)).then_inc(sems["b1tt1"], 1)
            W.w(v, "v", sems["b1rot"], u + 1)
            v.scalar_tensor_tensor(_sl(c2B, b * 512, 512), ps[2 + b][:], 1.0,
                                   _sl(sinQ, tt * 512, 512), OP.mult, OP.mult)
            v.tensor_add(_sl(qT, oc * 1024 + tt * 512, 512),
                         _sl(c1B, b * 512, 512),
                         _sl(c2B, b * 512, 512)).then_inc(sems["b1cmb"], 1)

    B1_DONE = [("b1mm", 64), ("b1cmb", 64), ("b1qraw", 64)]
    CHN = [16] * 8 + [32] * 8
    CUM = [0]
    for _c in CHN:
        CUM.append(CUM[-1] + _c)
    TCH = CUM[16]

    def phB2_y(y):
        for sname, val in B1_DONE:
            W.w(y, "y", sems[sname], val)
        for kv in range(16):
            bl, kvh = kv // 8, kv % 8
            if kv >= 2:
                W.w(y, "y", sems["b2sc"], 4 * CUM[kv - 1])
            y.dma_start(out=_sl(Kb, (kv % 2) * 4096, 4096),
                        in_=kdram[kvh * 128:(kvh + 1) * 128, :]
                        ).then_inc(sems["b2k%d" % (kv % 2)], 16)
            if kv >= 2:
                W.w(y, "y", sems["b2pv"], 4 * CUM[kv - 1])
            y.dma_start(out=_sl(Vb, (kv % 2) * 4096, 4096),
                        in_=vdram[kvh:kvh + 1, :, :, :]
                        ).then_inc(sems["b2v%d" % (kv % 2)], 16)
            for c in range(CHN[kv]):
                mi = CUM[kv] + c
                W.w(y, "y", sems["b2pl"], 4 * (mi - 1))
                y.dma_start(out=_sl(mkb, (c % 2) * 512, 512),
                            in_=dt["masks"][(bl * 32 + c) * 128:
                                            (bl * 32 + c + 1) * 128, :]
                            ).then_inc(sems["b2m%d" % (mi % 2)], 16)
        if DEBUG_TAPS:
            W.w(y, "y", sems["b1cmb"], 64)
            y.dma_start(out=taps["tapQ"][:], in_=qT[:]).then_inc(sems["stap"], 16)
            W.w(y, "y", sems["b2stt"], 64)
            y.dma_start(out=taps["tapOT"][:], in_=arena2[:]).then_inc(sems["stap"], 16)

    def phB2_t(t):
        W.w(t, "t", sems["b1qraw"], 64)
        W.w(t, "t", sems["b1cmb"], 64)
        for kv in range(16):
            bl, kvh = kv // 8, kv % 8
            W.w(t, "t", sems["b2k%d" % (kv % 2)], 16 * (kv // 2 + 1))
            W.w(t, "t", sems["b2v%d" % (kv % 2)], 16 * (kv // 2 + 1))
            nch = CHN[kv]

            def _bcasts(kvp, exdrain):
                W.w(t, "t", sems["b2ex"], exdrain)
                for qi in range(4):
                    W.w(t, "t", sems["b2rc"], kvp + 1)
                    W.w(t, "t", sems["b2bb"], 4 * kvp + qi)
                    t.matmul(ps[7][:], ebc[:, qi * 128:qi * 128 + 128], inv4[:],
                             start=True, stop=True).then_inc(sems["b2bc"], 1)

            for c in range(nch + 1):
                if c < nch:
                    for qi in range(4):
                        qh = kvh * 4 + qi
                        u2 = (CUM[kv] + c) * 4 + qi
                        W.w(t, "t", sems["b2ex"], u2 - 2)
                        if c >= 2:
                            W.w(t, "t", sems["b2bb"], 4 * kv)
                        t.matmul(ps[(4, 5, 7)[u2 % 3]][:],
                                 _sl(Kb, (kv % 2) * 4096 + c * 128, 128),
                                 _sl(qT, qh * 1024 + bl * 512, 512),
                                 start=True, stop=True).then_inc(sems["b2sc"], 1)
                if c == 1 and kv >= 1:
                    _bcasts(kv - 1, 4 * (CUM[kv] + 2))
                if c >= 1:
                    cp = c - 1
                    for qi in range(4):
                        if cp == 0:
                            W.w(t, "t", sems["b2stt"], 4 * (kv - 1) + qi + 1)
                        W.w(t, "t", sems["b2pl"], 4 * (CUM[kv] + cp) + qi + 1)
                        t.matmul(ps[qi][:],
                                 _sl(Vb, (kv % 2) * 4096 + cp * 128, 128),
                                 _sl(Pm, qi * 512, 512),
                                 start=(cp == 0), stop=(cp == nch - 1)
                                 ).then_inc(sems["b2pv"], 1)
                    for qi in range(4):
                        t.matmul(ps[6][0:4, :], ones4v[:, qi * 4:qi * 4 + 4],
                                 _sl(Pm, qi * 512, 512),
                                 start=(cp == 0 and qi == 0),
                                 stop=(cp == nch - 1 and qi == 3)
                                 ).then_inc(sems["b2sm"], 1)
            if kv == 15:
                _bcasts(15, 4 * CUM[16])

    def phB2_s(s):
        for kv in range(16):
            for c in range(CHN[kv]):
                for qi in range(4):
                    u2 = (CUM[kv] + c) * 4 + qi
                    W.w(s, "s", sems["b2sc"], u2 + 1)
                    W.w(s, "s", sems["b2pl"], 4 * (CUM[kv] + c - 1) + qi + 1)
                    s.activation(_sl(Pt, qi * 512, 512), ps[(4, 5, 7)[u2 % 3]][:],
                                 AF.Exp, bias=negC[:], scale=SCALE
                                 ).then_inc(sems["b2ex"], 1)

    def phB2_g(g):
        pass

    def phB2_v(v):
        def _vepi(kvp):
            blp, kvhp = kvp // 8, kvp % 8
            W.w(v, "v", sems["b2sm"], 4 * CUM[kvp + 1])
            W.w(v, "v", sems["b2bc"], 4 * kvp)
            v.reciprocal(inv4[:], ps[6][0:4, :]).then_inc(sems["b2rc"], 1)
            for qi in range(4):
                qh = kvhp * 4 + qi
                W.w(v, "v", sems["b2bc"], 4 * kvp + qi + 1)
                v.tensor_copy(_sl(invb, (qi % 2) * 512, 512),
                              ps[7][:]).then_inc(sems["b2bb"], 1)
                W.w(v, "v", sems["b2pv"], 4 * CUM[kvp + 1] - 4 + qi + 1)
                if kvp == 0 and qi == 0:
                    W.w(v, "v", sems["b1mm"], 64)
                v.scalar_tensor_tensor(
                    _sl(arena2, blp * 16384 + qh * 512, 512),
                    ps[qi][:], 1.0, _sl(invb, (qi % 2) * 512, 512),
                    OP.mult, OP.mult).then_inc(sems["b2stt"], 1)

        for kv in range(16):
            bl, kvh = kv // 8, kv % 8
            for c in range(CHN[kv]):
                for qi in range(4):
                    u2 = (CUM[kv] + c) * 4 + qi
                    W.w(v, "v", sems["b2ex"], u2 + 1)
                    mi_ = CUM[kv] + c
                    W.w(v, "v", sems["b2m%d" % (mi_ % 2)], 16 * (mi_ // 2 + 1))
                    W.w(v, "v", sems["b2pv"], 4 * (CUM[kv] + c - 1) + qi + 1)
                    W.w(v, "v", sems["b2sm"], 4 * (CUM[kv] + c - 1) + qi + 1)
                    v.tensor_mul(_sl(Pm, qi * 512, 512), _sl(Pt, qi * 512, 512),
                                 _sl(mkb, (c % 2) * 512, 512)
                                 ).then_inc(sems["b2pl"], 1)
                if c == 0 and kv >= 1:
                    _vepi(kv - 1)
        _vepi(15)

    B2_DONE = [("b2ex", 1536), ("b2stt", 64), ("b2rc", 16), ("b2bb", 64),
               ("b2pl", 1536), ("b2pv", 1536), ("b2sm", 1536), ("b2sc", 1536)]

    def phB3_g(g):
        for sname, val in B2_DONE:
            W.w(g, "g", sems[sname], val)
        if DEBUG_TAPS:
            W.w(g, "g", sems["stap"], 32)
        for h2 in range(64):
            hc = h2 % 32
            W.w(g, "g", sems["b3mm"], h2 - 1)
            g.dma_start(out=_sl(wo_ch, (h2 % 2) * 4096, 4096),
                        in_=dt["wo_h"][hc]).then_inc(sems["b3w"], 16)

    def phB3_y(y):
        for sname, val in B2_DONE:
            W.w(y, "y", sems[sname], val)
        for bl in range(2):
            for hc in range(32):
                h2 = bl * 32 + hc
                W.w(y, "y", sems["b3rs"], h2 - 1)
                y.dma_start(out=_sl(xf, (h2 % 2) * 512, 512),
                            in_=dt["xownT"][hc * 128:(hc + 1) * 128,
                                            bl * 512:(bl + 1) * 512]
                            ).then_inc(sems["b3x"], 16)
            for hc in range(32):
                h2 = bl * 32 + hc
                W.w(y, "y", sems["b3x1"], h2 + 1)
                y.dma_start(out=x1dram[hc * 128:(hc + 1) * 128,
                                       bl * 512:(bl + 1) * 512],
                            in_=_sl(x1f, (h2 % 2) * 512, 512)
                            ).then_inc(sems["b3xs"], 16)

    def _b3_lnsums(t, bl, hc):
        h2 = bl * 32 + hc
        W.w(t, "t", sems["b3rs"], h2 + 1)
        if hc == 0:
            W.w(t, "t", sems["b3me"], bl)
            W.w(t, "t", sems["b3e2"], bl)
        t.matmul(ps[2][0:1, :], onescol[:], _sl(resid, hc * 512, 512),
                 start=(hc == 0), stop=(hc == 31)).then_inc(sems["b3s1"], 1)
        W.w(t, "t", sems["b3sq"], h2 + 1)
        t.matmul(ps[3][0:1, :], onescol[:], _sl(sqv, (h2 % 2) * 512, 512),
                 start=(hc == 0), stop=(hc == 31)).then_inc(sems["b3s2"], 1)

    def phB3_t(t):
        for sname, val in B2_DONE:
            W.w(t, "t", sems[sname], val)
        for bl in range(2):
            for hc in range(32):
                h2 = bl * 32 + hc
                W.w(t, "t", sems["b3w"], 16 * (h2 + 1))
                W.w(t, "t", sems["b3rs"], h2 - 1)
                for oc in range(32):
                    ins = t.matmul(
                        ps[h2 % 2][:],
                        _sl(wo_ch, (h2 % 2) * 4096 + oc * 128, 128),
                        _sl(arena2, bl * 16384 + oc * 512, 512),
                        start=(oc == 0), stop=(oc == 31))
                ins.then_inc(sems["b3mm"], 1)
                if hc >= 1:
                    _b3_lnsums(t, bl, hc - 1)
            _b3_lnsums(t, bl, 31)
            W.w(t, "t", sems["b3me"], bl + 1)
            W.w(t, "t", sems["b3mbe"], 2 * bl)
            t.matmul(ps[4][:], ones1f[:], mS[:],
                     start=True, stop=True).then_inc(sems["b3bc"], 1)
            W.w(t, "t", sems["b3rstd"], bl + 1)
            t.matmul(ps[5][:], ones1f[:], rstdS[:],
                     start=True, stop=True).then_inc(sems["b3bc"], 1)

    def phB3_s(s):
        for bl in range(2):
            W.w(s, "s", sems["b3s1"], 32 * (bl + 1))
            s.activation(mS[:], ps[2][0:1, :], AF.Identity,
                         scale=1.0 / D).then_inc(sems["b3me"], 1)
            W.w(s, "s", sems["b3s2"], 32 * (bl + 1))
            s.activation(tA5[:], ps[3][0:1, :], AF.Identity,
                         scale=1.0 / D).then_inc(sems["b3e2"], 1)
            W.w(s, "s", sems["b3var"], bl + 1)
            s.activation(tB5[:], tA5[:], AF.Ln, bias=eps1[0:1, :])
            s.activation(rstdS[:], tB5[:], AF.Exp,
                         scale=-0.5).then_inc(sems["b3rstd"], 1)
            W.w(s, "s", sems["b3bc"], 2 * bl + 1)
            s.activation(mb2[:], ps[4][:], AF.Identity).then_inc(sems["b3mbe"], 1)
            W.w(s, "s", sems["b3bc"], 2 * bl + 2)
            s.activation(rb2[:], ps[5][:], AF.Identity).then_inc(sems["b3mbe"], 1)

    def phB3_v(v):
        for bl in range(2):
            for hc in range(32):
                h2 = bl * 32 + hc
                W.w(v, "v", sems["b3mm"], h2 + 1)
                W.w(v, "v", sems["b3x"], 16 * (h2 + 1))
                if bl == 1 and hc == 0:
                    W.w(v, "v", sems["b3x1"], 32)
                v.tensor_add(_sl(resid, hc * 512, 512), ps[h2 % 2][:],
                             _sl(xf, (h2 % 2) * 512, 512)
                             ).then_inc(sems["b3rs"], 1)
                W.w(v, "v", sems["b3s2"], h2 - 1)
                v.tensor_mul(_sl(sqv, (h2 % 2) * 512, 512),
                             _sl(resid, hc * 512, 512),
                             _sl(resid, hc * 512, 512)).then_inc(sems["b3sq"], 1)
            W.w(v, "v", sems["b3me"], bl + 1)
            vsync(v, v.tensor_mul(tB5[:], mS[:], mS[:]))
            W.w(v, "v", sems["b3e2"], bl + 1)
            v.tensor_sub(tA5[:], tA5[:], tB5[:]).then_inc(sems["b3var"], 1)
            W.w(v, "v", sems["b3mbe"], 2 * bl + 2)
            W.w(v, "v", sems["b3mm"], 32 * (bl + 1))
            for hc in range(32):
                h2 = bl * 32 + hc
                v.scalar_tensor_tensor(_sl(t1f, (h2 % 2) * 512, 512),
                                       _sl(resid, hc * 512, 512), 1.0,
                                       mb2[:], OP.mult, OP.subtract)
                W.w(v, "v", sems["b3xs"], 16 * (h2 - 1))
                v.tensor_mul(_sl(x1f, (h2 % 2) * 512, 512),
                             _sl(t1f, (h2 % 2) * 512, 512), rb2[:])
                v.tensor_copy(_sl(arena2, bl * 16384 + hc * 512, 512),
                              _sl(x1f, (h2 % 2) * 512, 512)
                              ).then_inc(sems["b3x1"], 1)

    B3_DONE = [("b3s1", 64), ("b3s2", 64), ("b3xs", 1024), ("b3x1", 64),
               ("b3mbe", 4)]

    def phB4_g(g):
        for sname, val in B3_DONE:
            W.w(g, "g", sems[sname], val)
        g.dma_start(out=bcat[:], in_=dt["bcat_h"][:]).then_inc(sems["s0"], 16)
        g.dma_start(out=acat[:], in_=dt["acat_h"][:]).then_inc(sems["s0"], 16)
        g.dma_start(out=rw[:], in_=dt["rw_h"][:]).then_inc(sems["s0"], 16)
        for bl in range(2):
            for oc in range(32):
                o2 = bl * 32 + oc
                W.w(g, "g", sems["b4mm"], o2 - 1)
                g.dma_start(out=_sl(wb_ch, (o2 % 2) * 4096, 4096),
                            in_=dt["wb_h"][oc]).then_inc(sems["b4wb"], 16)

    def phB4_y(y):
        for sname, val in B3_DONE:
            W.w(y, "y", sems[sname], val)
        y.dma_start(out=bbt[:], in_=dt["bb_h"][:]).then_inc(sems["s0"], 16)
        for bl in range(2):
            for oc in range(32):
                o2 = bl * 32 + oc
                W.w(y, "y", sems["b4r2"], o2 - 1)
                W.w(y, "y", sems["b3xs"], 16 * (bl * 32 + oc + 1))
                y.dma_start(out=_sl(xf1, (o2 % 2) * 512, 512),
                            in_=x1dram[oc * 128:(oc + 1) * 128,
                                       bl * 512:(bl + 1) * 512]
                            ).then_inc(sems["b4x"], 16)
            for hc in range(32):
                h2 = bl * 32 + hc
                W.w(y, "y", sems["b4o"], h2 + 1)
                y.dma_start(out=out_d[hc * 128:(hc + 1) * 128,
                                      bl * 512:(bl + 1) * 512],
                            in_=_sl(outf, (h2 % 2) * 512, 512)
                            ).then_inc(sems["sout"], 16)
        if DEBUG_TAPS:
            y.dma_start(out=taps["tapA"][:, 0:1], in_=mask16[:], allow_slow_non_contiguous=True).then_inc(sems["stap"], 16)
            y.dma_start(out=taps["tapA"][:, 1:2], in_=biasv[:], allow_slow_non_contiguous=True).then_inc(sems["stap"], 16)
            y.dma_start(out=taps["tapA"][:, 2:3], in_=lnp16[:], allow_slow_non_contiguous=True).then_inc(sems["stap"], 16)
            y.dma_start(out=taps["tapA"][:, 3:4], in_=t16a[:], allow_slow_non_contiguous=True).then_inc(sems["stap"], 16)
            y.dma_start(out=taps["tapA"][:, 4:5], in_=t16b[:], allow_slow_non_contiguous=True).then_inc(sems["stap"], 16)
            y.dma_start(out=taps["tapA"][:, 5:9], in_=eq16[:], allow_slow_non_contiguous=True).then_inc(sems["stap"], 16)
            y.dma_start(out=taps["tapA"][0:1, 9:13], in_=act_f[:], allow_slow_non_contiguous=True).then_inc(sems["stap"], 16)
            y.dma_start(out=taps["tapA"][:, 16:528], in_=se[:]).then_inc(sems["stap"], 16)
            y.dma_start(out=taps["tapK"][:], in_=kdram[:]).then_inc(sems["stap"], 16)
            y.dma_start(out=taps["tapV"][:], in_=vdram[:]).then_inc(sems["stap"], 16)
            W.w(y, "y", sems["b3xs"], 1024)
            y.dma_start(out=taps["tapX1"][:], in_=x1dram[:]).then_inc(sems["stap"], 16)
            W.w(y, "y", sems["stap"], 208)
        W.w(y, "y", sems["sout"], 16 * 64)

    def _b4_lnsums(t, bl, oc):
        o2 = bl * 32 + oc
        W.w(t, "t", sems["b4r2"], o2 + 1)
        t.matmul(ps[2][0:1, :], onescol[:], _sl(r2, oc * 512, 512),
                 start=(oc == 0), stop=(oc == 31)).then_inc(sems["b4s1"], 1)
        W.w(t, "t", sems["b4sq"], o2 + 1)
        t.matmul(ps[3][0:1, :], onescol[:], _sl(sq2, (o2 % 2) * 512, 512),
                 start=(oc == 0), stop=(oc == 31)).then_inc(sems["b4s2"], 1)

    def phB4_t(t):
        W.w(t, "t", sems["b3rs"], 64)
        W.w(t, "t", sems["b3me"], 2)
        W.w(t, "t", sems["b3e2"], 2)
        W.w(t, "t", sems["b3mbe"], 4)
        W.w(t, "t", sems["s0"], S0B4)
        for bl in range(2):
            W.w(t, "t", sems["b3x1"], 32 * (bl + 1))
            W.w(t, "t", sems["b4w"], bl)
            W.w(t, "t", sems["b4me"], bl)
            W.w(t, "t", sems["b4e2"], bl)
            for hc in range(HC):
                ins = t.matmul(ps[2][0:16, :], rw[:, hc * 16:hc * 16 + 16],
                               _sl(arena2, bl * 16384 + hc * 512, 512),
                               start=(hc == 0), stop=(hc == HC - 1))
            ins.then_inc(sems["b4r"], 1)
            W.w(t, "t", sems["b4ze"], bl)
            for hc in range(HC):
                ins = t.matmul(ps[3][:], _sl(acat, hc * 128, 128),
                               _sl(arena2, bl * 16384 + hc * 512, 512),
                               start=(hc == 0), stop=(hc == HC - 1))
            ins.then_inc(sems["b4z"], 1)
            W.w(t, "t", sems["b4se"], bl + 1)
            W.w(t, "t", sems["b4iv"], bl)
            t.matmul(ps[4][0:1, :], ones16[:], se[:],
                     start=True, stop=True).then_inc(sems["b4sm"], 1)
            W.w(t, "t", sems["b4iv"], bl + 1)
            W.w(t, "t", sems["b4al"], bl)
            t.matmul(ps[5][0:16, :], ones1f[:, 0:16], inv1[:],
                     start=True, stop=True).then_inc(sems["b4b16"], 1)
            W.w(t, "t", sems["b4al"], bl + 1)
            t.matmul(ps[6][:], indm[:], alpha[:],
                     start=True, stop=True).then_inc(sems["b4aem"], 1)
            for oc in range(32):
                o2 = bl * 32 + oc
                W.w(t, "t", sems["b4wb"], 16 * (o2 + 1))
                W.w(t, "t", sems["b4ye"], o2 - 1)
                if oc == 0:
                    W.w(t, "t", sems["b4w"], bl + 1)
                for hc in range(HC):
                    t.matmul(ps[o2 % 2][:],
                             _sl(wb_ch, (o2 % 2) * 4096 + hc * 128, 128),
                             _sl(arena2, bl * 16384 + hc * 512, 512),
                             start=(hc == 0), stop=False)
                t.matmul(ps[o2 % 2][:], _sl(bcat, oc * 128, 128),
                         _sl(wbf, (bl % 2) * 512, 512),
                         start=False, stop=True).then_inc(sems["b4mm"], 1)
                if oc >= 1:
                    _b4_lnsums(t, bl, oc - 1)
            _b4_lnsums(t, bl, 31)
            W.w(t, "t", sems["b4me"], bl + 1)
            W.w(t, "t", sems["b4mbe"], 2 * bl)
            W.w(t, "t", sems["b4iv"], bl + 1)
            t.matmul(ps[4][:], ones1f[:], mS[:],
                     start=True, stop=True).then_inc(sems["b4bc"], 1)
            W.w(t, "t", sems["b4rstd"], bl + 1)
            t.matmul(ps[5][:], ones1f[:], rstdS[:],
                     start=True, stop=True).then_inc(sems["b4bc"], 1)

    def phB4_s(s):
        W.w(s, "s", sems["sp_v"], 11)
        for bl in range(2):
            W.w(s, "s", sems["b4r"], bl + 1)
            s.activation(se[:], ps[2][0:16, :], AF.Exp,
                         bias=biasv[:]).then_inc(sems["b4se"], 1)
            W.w(s, "s", sems["b4z"], bl + 1)
            s.activation(zsb[:], ps[3][:], AF.Identity).then_inc(sems["b4ze"], 1)
            for oc in range(32):
                o2 = bl * 32 + oc
                W.w(s, "s", sems["b4mm"], o2 + 1)
                W.w(s, "s", sems["b4r2"], o2 - 1)
                s.activation(_sl(fY, (o2 % 2) * 512, 512), ps[o2 % 2][:],
                             AF.Identity, bias=bbt[:, oc:oc + 1]
                             ).then_inc(sems["b4ye"], 1)
            W.w(s, "s", sems["b4s1"], 32 * (bl + 1))
            s.activation(mS[:], ps[2][0:1, :], AF.Identity,
                         scale=1.0 / D).then_inc(sems["b4me"], 1)
            W.w(s, "s", sems["b4s2"], 32 * (bl + 1))
            s.activation(tA5[:], ps[3][0:1, :], AF.Identity,
                         scale=1.0 / D).then_inc(sems["b4e2"], 1)
            W.w(s, "s", sems["b4var"], bl + 1)
            s.activation(tB5[:], tA5[:], AF.Ln, bias=eps1[0:1, :])
            s.activation(rstdS[:], tB5[:], AF.Exp,
                         scale=-0.5).then_inc(sems["b4rstd"], 1)
            W.w(s, "s", sems["b4bc"], 2 * bl + 1)
            s.activation(mb4[:], ps[4][:], AF.Identity).then_inc(sems["b4mbe"], 1)
            W.w(s, "s", sems["b4bc"], 2 * bl + 2)
            s.activation(rb4[:], ps[5][:], AF.Identity).then_inc(sems["b4mbe"], 1)

    def phB4_v(v):
        for bl in range(2):
            W.w(v, "v", sems["b4sm"], bl + 1)
            W.w(v, "v", sems["b4b16"], bl)
            v.reciprocal(inv1[:], ps[4][0:1, :]).then_inc(sems["b4iv"], 1)
            W.w(v, "v", sems["b4b16"], bl + 1)
            W.w(v, "v", sems["b4aem"], bl)
            v.tensor_mul(alpha[:], se[:], ps[5][0:16, :]).then_inc(sems["b4al"], 1)
            W.w(v, "v", sems["b4aem"], bl + 1)
            W.w(v, "v", sems["b4ze"], bl + 1)
            v.scalar_tensor_tensor(_sl(wbf, (bl % 2) * 512, 512),
                                   zsb[:], LORA_SCALING, ps[6][:],
                                   OP.mult, OP.mult).then_inc(sems["b4w"], 1)
            for oc in range(32):
                o2 = bl * 32 + oc
                W.w(v, "v", sems["b4ye"], o2 + 1)
                W.w(v, "v", sems["b4x"], 16 * (o2 + 1))
                if bl == 1 and oc == 0:
                    W.w(v, "v", sems["b4o"], 32)
                v.tensor_add(_sl(r2, oc * 512, 512),
                             _sl(fY, (o2 % 2) * 512, 512),
                             _sl(xf1, (o2 % 2) * 512, 512)
                             ).then_inc(sems["b4r2"], 1)
                W.w(v, "v", sems["b4s2"], o2 - 1)
                v.tensor_mul(_sl(sq2, (o2 % 2) * 512, 512),
                             _sl(r2, oc * 512, 512),
                             _sl(r2, oc * 512, 512)).then_inc(sems["b4sq"], 1)
            W.w(v, "v", sems["b4me"], bl + 1)
            vsync(v, v.tensor_mul(tB5[:], mS[:], mS[:]))
            W.w(v, "v", sems["b4e2"], bl + 1)
            v.tensor_sub(tA5[:], tA5[:], tB5[:]).then_inc(sems["b4var"], 1)
            W.w(v, "v", sems["b4mbe"], 2 * bl + 2)
            for hc in range(32):
                h2 = bl * 32 + hc
                v.scalar_tensor_tensor(_sl(t1g, (h2 % 2) * 512, 512),
                                       _sl(r2, hc * 512, 512), 1.0,
                                       mb4[:], OP.mult, OP.subtract)
                W.w(v, "v", sems["sout"], 16 * (h2 - 1))
                v.tensor_mul(_sl(outf, (h2 % 2) * 512, 512),
                             _sl(t1g, (h2 % 2) * 512, 512),
                             rb4[:]).then_inc(sems["b4o"], 1)

    with nc.Block() as blk:

        @blk.sync
        def _(y):
            ph0_sync(y)
            phA1_y(y)
            phA2_y(y)
            phB1_y(y)
            phB2_y(y)
            phB3_y(y)
            phB4_y(y)

        @blk.gpsimd
        def _(g):
            phA1_g(g)
            phA2_g(g)
            phB1_g(g)
            phB2_g(g)
            phB3_g(g)
            phB4_g(g)

        @blk.tensor
        def _(t):
            phP_t(t)
            phA1_t(t)
            phA2_t(t)
            phB1_t(t)
            phB2_t(t)
            phB3_t(t)
            phB4_t(t)

        @blk.scalar
        def _(s):
            phP_s(s)
            phA1_s(s)
            phA2_s(s)
            phB1_s(s)
            phB2_s(s)
            phB3_s(s)
            phB4_s(s)

        @blk.vector
        def _(v):
            phP_v(v)
            phA1_v(v)
            phB1_v(v)
            phB2_v(v)
            phB3_v(v)
            phB4_v(v)

    return nc, taps


_CACHE = {}


def kernel(**inputs):
    maps = _prep_in_maps(inputs)
    if "nc" not in _CACHE:
        _CACHE["nc"] = build_program()
    nc, taps = _CACHE["nc"]
    res = run_bass_kernel_spmd(nc, maps, core_ids=list(range(8)))
    y = np.empty((2, S, D), np.float32)
    for c in range(8):
        b, q0a, q0b = _core_blocks(c)
        sh = res.results[c]["out"]
        y[b, q0a:q0a + 512, :] = sh[:, 0:512].T
        y[b, q0b:q0b + 512, :] = sh[:, 512:1024].T
    if DEBUG_TAPS:
        kernel.last_results = res
    return y
